# version 15
# baseline (speedup 1.0000x reference)
import contextlib
import numpy as np
import ml_dtypes
import concourse.bass as bass
import concourse.mybir as mybir
from concourse.bass_utils import run_bass_kernel_spmd

F32 = mybir.dt.float32
BF16 = mybir.dt.bfloat16
AF = mybir.ActivationFunctionType
ALU = mybir.AluOpType
AX = mybir.AxisListType

SEQ = 4096
D = 1024
H = 16
NT = 32
NS = 8
EPS = 1e-5
BATCHED_EXP_H0 = 3

SLOPES = [2.0 ** (-(h + 1) / 2.0) for h in range(H)]
ENGS = ("tensor", "scalar", "vector", "gpsimd", "sync")
ESEM = {"tensor": "pe", "scalar": "act", "vector": "dve", "gpsimd": "pool"}


def _host_consts():
    p = np.arange(128, dtype=np.float32)[:, None]
    ident = np.eye(128, dtype=np.float32).astype(ml_dtypes.bfloat16)
    id8 = np.eye(8, dtype=np.float32)
    j = np.arange(128, dtype=np.float32)[None, :]
    m1 = (p >= j).astype(np.float32).astype(ml_dtypes.bfloat16)
    m2 = (j >= p).astype(np.float32).astype(ml_dtypes.bfloat16)
    biasj = np.zeros((128, 80), np.float32)
    for h in range(H):
        for jj in range(5):
            biasj[:, h * 5 + jj] = SLOPES[h] * (p[:, 0] - 128.0 * jj - 64.0)
    biasr = np.zeros((128, 128), np.float32)
    for h in range(H):
        for r in range(-4, 4):
            biasr[:, h * 8 + r + 4] = SLOPES[h] * (p[:, 0] + 128.0 * r - 256.0)
    return ident, id8, m1, m2, biasj, biasr


class Prog:

    def __init__(self, nc, sems):
        self.nc = nc
        self.sem = sems
        self.cnt = {k: 0 for k in sems}
        self.q = {e: [] for e in ENGS}
        self.mark = {e: {k: 0 for k in sems} for e in ENGS}
        self.res = {}
        self.pending = {e: [] for e in ENGS}

    def _edge(self, waccum, eng, semkey, val, skip_same=False):
        if skip_same:
            return
        waccum.append((semkey, val))

    def op(self, eng, fn, rd=(), wr=(), inc="eng", step=1, waits=(),
           no_self_raw=()):
        if inc == "eng":
            semkey = ESEM[eng]
        elif inc is None:
            semkey = None
        else:
            semkey = inc
        w = list(waits)
        for t in rd:
            e = self.res.get(t)
            if e is None:
                continue
            assert e["w"] != "pending", f"read of pending-write token {t}"
            if e["w"] is not None:
                weng, wsem, wval = e["w"]
                same = (weng == eng and wsem in ESEM.values())
                if same and (weng == "tensor" or t in no_self_raw):
                    pass
                else:
                    w.append((wsem, wval))
        for t in wr:
            e = self.res.get(t)
            if e is None:
                continue
            if e["w"] is not None and e["w"] != "pending":
                weng, wsem, wval = e["w"]
                same_q = (weng == eng and (wsem in ESEM.values() or wsem == semkey))
                if not same_q:
                    w.append((wsem, wval))
            for (reng, rsem), rval in e["r"].items():
                same_q = (reng == eng and rsem in ESEM.values())
                if not same_q:
                    w.append((rsem, rval))
        ww = []
        for k, v in w:
            if v > self.mark[eng][k]:
                ww.append((k, v))
                self.mark[eng][k] = v
        tag = None
        if semkey is not None:
            self.cnt[semkey] += step
            tag = (semkey, step)
        self.q[eng].append((ww, fn, tag))
        val = self.cnt[semkey] if semkey is not None else None
        if semkey is None:
            for t in wr:
                self.pending[eng].append(t)
                self.res[t] = {"w": "pending", "r": {}}
        else:
            for t in list(self.pending[eng]) + list(wr):
                self.res[t] = {"w": (eng, semkey, val), "r": {}}
            self.pending[eng] = []
            for t in rd:
                e = self.res.setdefault(t, {"w": None, "r": {}})
                e["r"][(eng, semkey)] = val
        return val

    def flush(self):
        live = {e: lst for e, lst in self.q.items() if lst}
        if live:
            with self.nc.Block() as blk:
                for ename, lst in live.items():
                    def section(eng, lst=lst):
                        for w, fn, tag in lst:
                            for k, v in w:
                                eng.wait_ge(self.sem[k], v)
                            instr = fn(eng)
                            if tag is not None:
                                instr.then_inc(self.sem[tag[0]], tag[1])
                    getattr(blk, ename)(section)
        self.q = {e: [] for e in ENGS}
        for e in ENGS:
            for k in ("pe", "act", "dve", "pool"):
                self.mark[e][k] = max(self.mark[e][k], self.cnt[k])


class Ctx:
    pass


def sbanks(a, w):
    return tuple(f"ps{b}" for b in range(a // 512, (a + w - 1) // 512 + 1))


def build(ns=NS):
    nc = bass.Bass("TRN2", target_bir_lowering=False,
                   detect_race_conditions=False)

    dr = {}
    dr["x"] = nc.dram_tensor("x", [SEQ, D], F32, kind="ExternalInput")
    dr["wkqv"] = nc.dram_tensor("wkqv", [D, 3 * D], F32, kind="ExternalInput")
    dr["wr"] = nc.dram_tensor("wr", [D, D], F32, kind="ExternalInput")
    dr["wk"] = nc.dram_tensor("wk", [D, D], F32, kind="ExternalInput")
    dr["wv"] = nc.dram_tensor("wv", [D, D], F32, kind="ExternalInput")
    dr["mur"] = nc.dram_tensor("mur", [D], F32, kind="ExternalInput")
    dr["muk"] = nc.dram_tensor("muk", [D], F32, kind="ExternalInput")
    dr["identc"] = nc.dram_tensor("identc", [128, 128], BF16, kind="ExternalInput")
    dr["id8c"] = nc.dram_tensor("id8c", [8, 8], F32, kind="ExternalInput")
    dr["m1c"] = nc.dram_tensor("m1c", [128, 128], BF16, kind="ExternalInput")
    dr["m2c"] = nc.dram_tensor("m2c", [128, 128], BF16, kind="ExternalInput")
    dr["biasjc"] = nc.dram_tensor("biasjc", [128, 80], F32, kind="ExternalInput")
    dr["biasrc"] = nc.dram_tensor("biasrc", [128, 128], F32, kind="ExternalInput")
    dr["out"] = nc.dram_tensor("out", [SEQ, D], F32, kind="ExternalOutput")

    c = Ctx()
    c.nc = nc
    c.dr = dr

    with contextlib.ExitStack() as _es:
        def sb(name, shape, dtype):
            return _es.enter_context(nc.sbuf_tensor(name, shape, dtype))
        c.WKQV = sb("WKQV", [128, 8 * 3072], BF16)
        c.WTR = sb("WTR", [128, 8 * 1024], BF16)
        c.WTK = sb("WTK", [128, 8 * 1024], BF16)
        c.WTV = sb("WTV", [128, 8 * 1024], BF16)
        c.KC = sb("KC", [128, 8 * 1024], BF16)
        c.VC = sb("VC", [128, 8 * 1040], BF16)
        c.QB = sb("QB", [128, 8 * 512], BF16)
        c.XRES = sb("XRES", [128, 4 * 1024], F32)
        c.LNKA = sb("LNKA", [128, 8 * 513], BF16)
        c.XRT = sb("XRT", [128, 8 * 512], BF16)
        c.XKT = sb("XKT", [128, 8 * 512], BF16)
        c.RBUF = sb("RBUF", [128, 1024], BF16)
        c.LNB = sb("LNB", [128, 2 * 1024], BF16)
        c.SCR = sb("SCR", [128, 7168], BF16)
        c.EWS = sb("EWS", [128, 3584], BF16)
        c.MU = sb("MU", [128, 16], F32)
        c.IDENT = sb("IDENT", [128, 128], BF16)
        c.ID8 = sb("ID8", [8, 8], F32)
        c.M1 = sb("M1", [128, 128], BF16)
        c.M2 = sb("M2", [128, 128], BF16)
        c.BIASJ = sb("BIASJ", [128, 80], F32)
        c.BIASR = sb("BIASR", [128, 128], F32)
        c.EPSC = sb("EPSC", [128, 1], F32)
        c.STT = sb("STT", [128, 48], F32)
        c.RS4 = sb("RS4", [128, 8], F32)
        c.SAVE1 = sb("SAVE1", [128, 8], BF16)
        c.SAVE3 = sb("SAVE3", [128, 8], BF16)
        c.PS_S = _es.enter_context(nc.psum_tensor("PS_S", [128, 2560], F32))
        c.PS_A = _es.enter_context(nc.psum_tensor("PS_A", [128, 512], F32))
        c.PS_B = _es.enter_context(nc.psum_tensor("PS_B", [128, 512], F32))
        c.PS_X = _es.enter_context(nc.psum_tensor("PS_X", [128, 512], F32))
        sems = {}
        for sname in ("scd", "sxd0", "sxd1", "sxd2", "sxd3", "sstg0", "sstg1",
                      "po0", "po1", "po2", "po3", "pe", "act", "dve", "pool"):
            sems[sname] = _es.enter_context(nc.semaphore("s_" + sname))
        c.P = Prog(nc, sems)

        c.MUS = c.SCR[0:8, 0:512].bitcast(F32)
        c.LNT = c.LNKA[:].rearrange("p (c w) -> p c w", w=513)
        c.PT = [c.SCR[:, 0:2560], c.SCR[:, 2560:5120]]
        c.PTtok = [("scrA", "scrB"), ("scrC", "scrD")]
        c.STG = [c.SCR[:, 0:2048].bitcast(F32), c.SCR[:, 2048:4096].bitcast(F32)]
        c.STGtok = [("scrA",), ("scrB", "scrC")]
        c.WSTG = c.SCR[:, 4096:5120]
        c.SQD = c.SCR[:, 5120:7168].bitcast(F32)
        c.EW_D = c.EWS[:, 0:512]
        c.KSC = [c.EWS[:, 512:1536].bitcast(F32), c.EWS[:, 1536:2560].bitcast(F32)]
        c.KSCtok = ["ewb", "ewc"]
        c.TH = c.EWS[:, 512:1536]
        c.FV = [c.EWS[:, 1536:2560], c.EWS[:, 2560:3584]]
        c.FVtok = ["ewc", "ewf"]

        build_program(c, ns)
    return nc


def build_program(c, ns):
    P = c.P
    preamble(c)
    weights(c)
    for s in range(ns):
        phase_ln(c, s, 1)
        phase_mix_k(c, s)
        phase_rv(c, s, store=False)
        phase_ln(c, s, 2)
        phase_kqv_kq(c, s)
        phase_attn(c, s, h_from=0, h_to=2, scores_only=True)
        phase_kqv_v(c, s)
        phase_attn(c, s, h_from=2, h_to=H)
        phase_ln(c, s, 3)
        phase_mix_k(c, s)
        phase_rv(c, s, store=True, refill=(s + 1 < ns))
    P.op("gpsimd", lambda g: g.engine_nop(),
         waits=[(f"po{tt}", P.cnt[f"po{tt}"]) for tt in range(4)], inc=None)
    P.flush()


def cdma(P, eng, dst, src, sem, rd=(), wr=()):
    return P.op(eng, lambda e, d=dst, s=src: e.dma_start(d, s), inc=sem,
                step=16, rd=rd, wr=wr)


def preamble(c):
    P, dr = c.P, c.dr
    for nm, tok in (("identc", "ident"), ("id8c", "id8"), ("m1c", "m1"),
                    ("m2c", "m2"), ("biasjc", "biasj"), ("biasrc", "biasr")):
        dst = {"ident": c.IDENT, "id8": c.ID8, "m1": c.M1, "m2": c.M2,
               "biasj": c.BIASJ, "biasr": c.BIASR}[tok]
        cdma(P, "sync", dst[:], dr[nm].ap(), "scd", wr=(tok,))
    cdma(P, "sync", c.MUS[0:8, 0:128],
         dr["mur"].ap().rearrange("(c p) -> c p", p=128), "scd", wr=("scrA",))
    cdma(P, "sync", c.MUS[0:8, 128:256],
         dr["muk"].ap().rearrange("(c p) -> c p", p=128), "scd", wr=("scrA",))
    for tt in range(4):
        cdma(P, "sync", c.XRES[:, tt * 1024:(tt + 1) * 1024],
             dr["x"].ap()[tt * 128:(tt + 1) * 128, :], f"sxd{tt}", wr=(f"x{tt}",))
    P.op("vector", lambda v: v.memset(c.EPSC[:], EPS), wr=("epsc",))
    vco = c.VC[:].rearrange("p (g w) -> p g w", w=65)
    P.op("vector", lambda v: v.memset(vco[:, :, 0:1], 1.0), wr=("vc",))
    P.op("vector", lambda v: v.memset(c.LNT[:, :, 0:1], 0.0), wr=("lnt",))
    P.op("vector", lambda v: v.memset(c.SAVE1[:], 0.0), wr=("save1",))
    P.op("vector", lambda v: v.memset(c.SAVE3[:], 0.0), wr=("save3",))
    P.op("tensor", lambda t: t.matmul(
        c.PS_A[:, 0:8], c.MUS[0:8, 0:128], c.ID8[:], start=True, stop=True),
        rd=("scrA", "id8"), wr=("ps5",))
    P.op("tensor", lambda t: t.matmul(
        c.PS_A[:, 8:16], c.MUS[0:8, 128:256], c.ID8[:], start=True, stop=True),
        rd=("scrA", "id8"), wr=("ps5",))
    P.op("vector", lambda v: v.tensor_copy(c.MU[:], c.PS_A[:, 0:16]),
         rd=("ps5",), wr=("mu",))


def weights(c):
    P, dr = c.P, c.dr
    wd = {0: dr["wr"], 1: dr["wk"], 2: dr["wv"]}
    wt = {0: (c.WTR, "wtr"), 1: (c.WTK, "wtk"), 2: (c.WTV, "wtv")}
    jobs = ([("ffn", w, dj) for w in (1, 0, 2) for dj in range(8)] +
            [("kqv", cc, t3) for cc in range(8) for t3 in range(3)])
    m = 0
    for n, job in enumerate(jobs):
        kind = job[0]
        stok = c.STGtok[n % 2]
        if kind == "kqv":
            _, cc, t3 = job
            src = dr["wkqv"].ap()[cc * 128:(cc + 1) * 128, t3 * 1024:(t3 + 1) * 1024]
        else:
            _, w, dj = job
            src = wd[w].ap()[dj * 128:(dj + 1) * 128, :]
        cdma(P, "sync", c.STG[n % 2][:], src, f"sstg{n % 2}", wr=stok)
        if kind == "kqv":
            _, cc, t3 = job
            dst = c.WKQV[:, cc * 3072 + t3 * 1024: cc * 3072 + (t3 + 1) * 1024]
            if t3 == 1:
                P.op("vector", lambda v, d_=dst, s_=c.STG[n % 2]:
                     v.tensor_scalar_mul(d_, s_[:], 1.0 / 32.0),
                     rd=stok, wr=("wkqv",))
            else:
                P.op("vector", lambda v, d_=dst, s_=c.STG[n % 2]:
                     v.tensor_copy(d_, s_[:]), rd=stok, wr=("wkqv",))
        else:
            _, w, dj = job
            if w == 2:
                P.op("vector", lambda v, s_=c.STG[n % 2]:
                     v.tensor_scalar_mul(c.WSTG[:], s_[:], 0.5),
                     rd=stok, wr=("scrD",))
            else:
                P.op("vector", lambda v, s_=c.STG[n % 2]:
                     v.tensor_copy(c.WSTG[:], s_[:]), rd=stok, wr=("scrD",))
            ps = c.PS_S[:, (m % 2) * 1024:(m % 2 + 1) * 1024]
            ptoks = sbanks((m % 2) * 1024, 1024)
            for ci in range(8):
                P.op("tensor", lambda t, ps_=ps, ci_=ci:
                     t.matmul(ps_[:, ci_ * 128:(ci_ + 1) * 128],
                              c.WSTG[:, ci_ * 128:(ci_ + 1) * 128],
                              c.IDENT[:], start=True, stop=True),
                     rd=("scrD", "ident"), wr=ptoks,
                     inc="eng" if ci == 7 else None)
            src2 = ps.rearrange("p (ci w2) -> p ci w2", w2=128)
            dst2 = wt[w][0][:].rearrange("p (ci w2) -> p ci w2", w2=1024)[:, :, dj * 128:(dj + 1) * 128]
            P.op("scalar", lambda a, d_=dst2, s_=src2: a.copy(d_, s_),
                 rd=ptoks, wr=(wt[w][1],))
            m += 1


def ln_tile_stats(c, tt):
    P = c.P
    ST = c.STT
    xk = f"x{tt}"
    xres = c.XRES[:, tt * 1024:(tt + 1) * 1024]
    P.op("scalar", lambda a, x_=xres, tt_=tt: a.activation(
        c.SQD[:], x_, AF.Square, accum_out=ST[:, 4 + tt_:5 + tt_]),
        rd=(xk,), wr=("scrE", f"sttS{tt}"))
    P.op("vector", lambda v, x_=xres, tt_=tt: v.reduce_sum(
        ST[:, tt_:tt_ + 1], x_, axis=AX.X), rd=(xk,), wr=(f"sttR{tt}",))


def phase_ln(c, s, which, skip_stats=False):
    P = c.P
    save, stok = {1: (c.SAVE1, "save1"), 2: (None, None),
                  3: (c.SAVE3, "save3")}[which]
    if which != 2 and s > 0:
        P.op("vector", lambda v, sv=save: v.tensor_copy(
            c.LNT[:, :, 0:1], sv[:].rearrange("p (cc w) -> p cc w", w=1)),
            rd=(stok,), wr=("lnt",))
    ST = c.STT
    if not skip_stats:
        for tt in range(4):
            ln_tile_stats(c, tt)
    sall = ("sttS0", "sttS1", "sttS2", "sttS3",
            "sttR0", "sttR1", "sttR2", "sttR3")
    P.op("vector", lambda v: v.scalar_tensor_tensor(
        ST[:, 8:12], ST[:, 0:4], -1.0 / (D * D), ST[:, 0:4],
        ALU.mult, ALU.mult), rd=sall, wr=("sttF",))
    P.op("vector", lambda v: v.scalar_tensor_tensor(
        ST[:, 12:16], ST[:, 4:8], 1.0 / D, ST[:, 8:12],
        ALU.mult, ALU.add), rd=sall + ("sttF",), wr=("sttF",))
    P.op("scalar", lambda a: a.activation(
        ST[:, 16:20], ST[:, 12:16], AF.Sqrt, bias=c.EPSC[:]),
        rd=("sttF", "epsc"), wr=("sttF",))
    P.op("vector", lambda v: v.reciprocal(
        ST[:, 20:24], ST[:, 16:20]), rd=("sttF",), wr=("sttF",))
    P.op("vector", lambda v: v.scalar_tensor_tensor(
        ST[:, 24:28], ST[:, 0:4], -1.0 / D, ST[:, 20:24],
        ALU.mult, ALU.mult), rd=sall + ("sttF",), wr=("sttF",))
    for tt in range(4):
        xk = f"x{tt}"
        xres = c.XRES[:, tt * 1024:(tt + 1) * 1024]
        ln = c.LNB[:, (tt % 2) * 1024:(tt % 2 + 1) * 1024]
        lt = f"lnb{tt % 2}"
        P.op("scalar", lambda a, x_=xres, tt_=tt, ln_=ln: a.activation(
            ln_, x_, AF.Identity, bias=ST[:, 24 + tt_:25 + tt_],
            scale=ST[:, 20 + tt_:21 + tt_]),
            rd=(xk, "sttF"), wr=(lt,))
        ps = c.PS_S[:, (tt % 2) * 1024:(tt % 2 + 1) * 1024]
        ptoks = sbanks((tt % 2) * 1024, 1024)
        for ci in range(8):
            P.op("tensor", lambda t, ps_=ps, ln_=ln, ci_=ci:
                 t.matmul(ps_[:, ci_ * 128:(ci_ + 1) * 128],
                          ln_[:, ci_ * 128:(ci_ + 1) * 128],
                          c.IDENT[:], start=True, stop=True),
                 rd=(lt, "ident"), wr=ptoks, inc="eng" if ci == 7 else None)
        src = ps.rearrange("p (ci w) -> p ci w", w=128)
        dst = c.LNT[:, :, 1 + tt * 128: 1 + (tt + 1) * 128]
        P.op("scalar", lambda a, d_=dst, s_=src: a.copy(d_, s_),
             rd=ptoks, wr=("lnt",))
    if save is not None:
        P.op("vector", lambda v, sv=save: v.tensor_copy(
            sv[:].rearrange("p (cc w) -> p cc w", w=1), c.LNT[:, :, 512:513]),
            rd=("lnt",), wr=(stok,))


def phase_mix_k(c, s):
    P = c.P
    d3 = c.XKT[:].rearrange("p (cc w) -> p cc w", w=512)
    xtoks = tuple(f"xkt{cc}" for cc in range(8))
    P.op("vector", lambda v: v.tensor_sub(
        d3, c.LNT[:, :, 1:513], c.LNT[:, :, 0:512]), rd=("lnt",), wr=xtoks)
    for cc in range(8):
        sh = c.LNT[:, cc, 0:512]
        P.op("vector", lambda v, sh_=sh, cc_=cc: v.scalar_tensor_tensor(
            c.XRT[:, cc_ * 512:(cc_ + 1) * 512],
            c.XKT[:, cc_ * 512:(cc_ + 1) * 512],
            c.MU[:, cc_:cc_ + 1], sh_, ALU.mult, ALU.add),
            rd=(f"xkt{cc}", "lnt", "mu"), wr=("xrt",))
        P.op("vector", lambda v, sh_=sh, cc_=cc: v.scalar_tensor_tensor(
            c.XKT[:, cc_ * 512:(cc_ + 1) * 512],
            c.XKT[:, cc_ * 512:(cc_ + 1) * 512],
            c.MU[:, 8 + cc_:9 + cc_], sh_, ALU.mult, ALU.add),
            rd=(f"xkt{cc}", "lnt", "mu"), wr=(f"xkt{cc}",))
    for dp in range(8):
        ps = (c.PS_A if dp % 2 == 0 else c.PS_B)[:, 0:512]
        btok = "ps5" if dp % 2 == 0 else "ps6"
        for cc in range(8):
            rdt = (("wtk",) + tuple(f"xkt{i}" for i in range(8))) if cc == 7 \
                else ("wtk", f"xkt{cc}")
            P.op("tensor", lambda t, ps_=ps, cc_=cc, dp_=dp: t.matmul(
                ps_, c.WTK[:, cc_ * 1024 + dp_ * 128: cc_ * 1024 + (dp_ + 1) * 128],
                c.XKT[:, cc_ * 512:(cc_ + 1) * 512],
                start=(cc_ == 0), stop=(cc_ == 7)),
                rd=rdt, wr=(btok,), inc="eng" if cc == 7 else None)
        kt = c.KSCtok[dp % 2]
        P.op("scalar", lambda a, ps_=ps, dp_=dp: a.activation(
            c.KSC[dp_ % 2][:, 0:512], ps_, AF.Relu), rd=(btok,), wr=(kt,))
        P.op("vector", lambda v, dp_=dp: v.tensor_mul(
            c.LNT[:, dp_, 1:513], c.KSC[dp_ % 2][:, 0:512],
            c.KSC[dp_ % 2][:, 0:512]), rd=(kt,), wr=("lnt",))


def phase_rv(c, s, store, refill=False, post_tile=None):
    P, dr = c.P, c.dr

    def regions(tt):
        if tt % 2 == 0:
            return ((c.PS_S[:, 0:512], "ps0"), (c.PS_S[:, 512:1024], "ps1"),
                    (c.PS_S[:, 1024:1536], "ps2"), (c.PS_S[:, 1536:2048], "ps3"))
        return ((c.PS_A[:, 0:512], "ps5"), (c.PS_B[:, 0:512], "ps6"),
                (c.PS_S[:, 2048:2560], "ps4"), (c.PS_X[:, 0:512], "ps7"))

    for tt in range(4):
        (r0, r0t), (r1, r1t), (v0, v0t), (v1, v1t) = regions(tt)
        for n, (ps, pt) in enumerate(((r0, r0t), (r1, r1t))):
            for cc in range(8):
                P.op("tensor", lambda t, ps_=ps, cc_=cc, n_=n, tt_=tt:
                     t.matmul(ps_,
                              c.XRT[:, cc_ * 512 + tt_ * 128: cc_ * 512 + tt_ * 128 + 128],
                              c.WTR[:, cc_ * 1024 + n_ * 512: cc_ * 1024 + (n_ + 1) * 512],
                              start=(cc_ == 0), stop=(cc_ == 7)),
                     rd=("xrt", "wtr"), wr=(pt,), inc="eng" if cc == 7 else None)
        for n, (ps, pt) in enumerate(((v0, v0t), (v1, v1t))):
            for cc in range(8):
                P.op("tensor", lambda t, ps_=ps, cc_=cc, n_=n, tt_=tt:
                     t.matmul(ps_,
                              c.LNT[:, cc_, 1 + tt_ * 128: 1 + tt_ * 128 + 128],
                              c.WTV[:, cc_ * 1024 + n_ * 512: cc_ * 1024 + (n_ + 1) * 512],
                              start=(cc_ == 0), stop=(cc_ == 7)),
                     rd=("lnt", "wtv"), wr=(pt,), inc="eng" if cc == 7 else None)
        P.op("scalar", lambda a, r0_=r0: a.activation(
            c.TH[:, 0:512], r0_, AF.Tanh, scale=0.5), rd=(r0t,), wr=("ewb",))
        P.op("scalar", lambda a, r1_=r1: a.activation(
            c.TH[:, 512:1024], r1_, AF.Tanh, scale=0.5), rd=(r1t,), wr=("ewb",))
        P.op("vector", lambda v: v.tensor_scalar(
            c.RBUF[:], c.TH[:], 0.5, 0.5, ALU.mult, ALU.add),
            rd=("ewb",), wr=("rbuf",))
        fv = c.FV[tt % 2]
        ft = c.FVtok[tt % 2]
        P.op("vector", lambda v, v0_=v0, fv_=fv: v.tensor_mul(
            fv_[:, 0:512], c.RBUF[:, 0:512], v0_), rd=("rbuf", v0t), wr=(ft,))
        P.op("vector", lambda v, v1_=v1, fv_=fv: v.tensor_mul(
            fv_[:, 512:1024], c.RBUF[:, 512:1024], v1_), rd=("rbuf", v1t), wr=(ft,))
        xres = c.XRES[:, tt * 1024:(tt + 1) * 1024]
        xk = f"x{tt}"
        P.op("vector", lambda g, x_=xres, fv_=fv: g.tensor_add(
            x_, x_, fv_[:]), rd=(xk, ft), wr=(xk,))
        if store:
            T = 4 * s + tt
            P.op("gpsimd", lambda g, x_=xres, T_=T: g.dma_start(
                dr["out"].ap()[T_ * 128:(T_ + 1) * 128, :], x_),
                inc=f"po{tt}", step=16, rd=(xk,))
            if refill:
                Tn = T + 4
                P.op("sync", lambda e, x_=xres, Tn_=Tn: e.dma_start(
                    x_, dr["x"].ap()[Tn_ * 128:(Tn_ + 1) * 128, :]),
                    inc=f"sxd{tt}", step=16, wr=(xk,))
        if post_tile is not None:
            post_tile(tt)


def phase_kqv_kq(c, s):
    P = c.P
    for part in range(2):
        for j in range(8):
            a0 = (part * 2 + j % 2) * 512
            ps = c.PS_S[:, a0:a0 + 512]
            btok = sbanks(a0, 512)
            for cc in range(8):
                P.op("tensor", lambda t, ps_=ps, cc_=cc, j_=j, p_=part:
                     t.matmul(ps_,
                              c.WKQV[:, cc_ * 3072 + p_ * 1024 + j_ * 128:
                                     cc_ * 3072 + p_ * 1024 + (j_ + 1) * 128],
                              c.LNT[:, cc_, 1:513],
                              start=(cc_ == 0), stop=(cc_ == 7)),
                     rd=("wkqv", "lnt"), wr=btok, inc="eng" if cc == 7 else None)
            if part == 0:
                dst = c.KC[:, j * 1024 + (s % 2) * 512: j * 1024 + (s % 2) * 512 + 512]
                dtok = "kc"
            else:
                dst = c.QB[:, j * 512:(j + 1) * 512]
                dtok = "qb"
            P.op("scalar", lambda a, d_=dst, ps_=ps: a.copy(d_, ps_),
                 rd=btok, wr=(dtok,))


def phase_kqv_v(c, s):
    P = c.P
    for tt in range(4):
        slot = (4 * s + tt) % 8
        pss = ((c.PS_A[:, 0:512], "ps5"), (c.PS_B[:, 0:512], "ps6"))
        vcs = c.VC[:, slot * 1040:(slot + 1) * 1040].rearrange("p (h w) -> p h w", w=65)
        for n, (ps, pt) in enumerate(pss):
            for cc in range(8):
                P.op("tensor", lambda t, ps_=ps, cc_=cc, n_=n, tt_=tt:
                     t.matmul(ps_,
                              c.LNT[:, cc_, 1 + tt_ * 128: 1 + tt_ * 128 + 128],
                              c.WKQV[:, cc_ * 3072 + 2048 + n_ * 512:
                                     cc_ * 3072 + 2048 + (n_ + 1) * 512],
                              start=(cc_ == 0), stop=(cc_ == 7)),
                     rd=("lnt", "wkqv"), wr=(pt,), inc="eng" if cc == 7 else None)
            src = ps.rearrange("p (h w) -> p h w", w=64)
            P.op("vector", lambda v, d_=vcs[:, 8 * n:8 * n + 8, 1:65], s_=src:
                 v.tensor_copy(d_, s_), rd=(pt,), wr=("vc",))


def attn_schedule(s):
    groups = []
    for kt in range(4 * s - 4, 4 * s + 4):
        if kt < 0:
            continue
        qlo = max(kt, 4 * s)
        qhi = min(kt + 4, 4 * s + 3)
        groups.append((kt, qlo, qhi))
    offs = []
    o = 0
    for (kt, qlo, qhi) in groups:
        offs.append(o)
        o += (qhi - qlo + 1) * 128
    return groups, offs


def phase_attn(c, s, h_from=0, h_to=H, scores_only=False):
    P = c.P
    groups, offs = attn_schedule(s)

    def emit_s(h):
        j = h >> 1
        half = (h & 1) * 64
        pieces = []
        for gi, (kt, qlo, qhi) in enumerate(groups):
            w = (qhi - qlo + 1) * 128
            a = offs[gi]
            while w > 0:
                wp = min(w, 512 - a % 512)
                pieces.append((kt, qlo, a, a - offs[gi], wp))
                a += wp
                w -= wp
        for pi, (kt, qlo, a, rel, wp) in enumerate(pieces):
            slot = kt % 8
            P.op("tensor", lambda t, a_=a, rel_=rel, wp_=wp, qlo_=qlo, slot_=slot:
                 t.matmul(c.PS_S[:, a_:a_ + wp_],
                          c.KC[half:half + 64,
                               j * 1024 + slot_ * 128: j * 1024 + (slot_ + 1) * 128],
                          c.QB[half:half + 64,
                               j * 512 + (qlo_ - 4 * s) * 128 + rel_:
                               j * 512 + (qlo_ - 4 * s) * 128 + rel_ + wp_],
                          start=True, stop=True, tile_position=(half, 0)),
                 rd=("kc", "qb"), wr=sbanks(a, wp),
                 inc="eng" if pi == len(pieces) - 1 else None)

    def emit_exp(h):
        pt = c.PT[h % 2]
        ptok = c.PTtok[h % 2]
        for gi, (kt, qlo, qhi) in enumerate(groups):
            if h < BATCHED_EXP_H0:
                for tt in range(qlo - 4 * s, qhi - 4 * s + 1):
                    jj = (4 * s + tt) - kt
                    o = offs[gi] + (tt - (qlo - 4 * s)) * 128
                    P.op("scalar", lambda a, o_=o, h_=h, jj_=jj: a.activation(
                        c.PT[h_ % 2][:, o_:o_ + 128], c.PS_S[:, o_:o_ + 128],
                        AF.Exp, bias=c.BIASJ[:, h_ * 5 + jj_: h_ * 5 + jj_ + 1]),
                        rd=sbanks(o, 128) + ("biasj",), wr=ptok)
            else:
                r = kt - 4 * s + 4
                w = (qhi - qlo + 1) * 128
                P.op("scalar", lambda a, gi_=gi, w_=w, h_=h, r_=r: a.activation(
                    c.PT[h_ % 2][:, offs[gi_]:offs[gi_] + w_],
                    c.PS_S[:, offs[gi_]:offs[gi_] + w_], AF.Exp,
                    bias=c.BIASR[:, h_ * 8 + r_: h_ * 8 + r_ + 1]),
                    rd=sbanks(offs[gi], w) + ("biasr",), wr=ptok)

    def emit_masks(h):
        ptok = c.PTtok[h % 2]
        for gi, (kt, qlo, qhi) in enumerate(groups):
            if kt < 4 * s and kt + 4 <= 4 * s + 3:
                o = offs[gi] + (kt + 4 - qlo) * 128
                P.op("vector", lambda g, o_=o, h_=h: g.tensor_mul(
                    c.PT[h_ % 2][:, o_:o_ + 128], c.PT[h_ % 2][:, o_:o_ + 128],
                    c.M1[:]), rd=ptok + ("m1",), wr=ptok)
            if kt >= 4 * s:
                o = offs[gi]
                P.op("vector", lambda g, o_=o, h_=h: g.tensor_mul(
                    c.PT[h_ % 2][:, o_:o_ + 128], c.PT[h_ % 2][:, o_:o_ + 128],
                    c.M2[:]), rd=ptok + ("m2",), wr=ptok)

    def emit_av(h):
        ps = c.PS_A if h % 2 == 0 else c.PS_B
        btok = "ps5" if h % 2 == 0 else "ps6"
        ptok = c.PTtok[h % 2]
        ops = []
        for tt in range(4):
            tq = 4 * s + tt
            kts = [g for g in groups if g[1] <= tq <= g[2]]
            for i, (kt, qlo, qhi) in enumerate(kts):
                gi = groups.index((kt, qlo, qhi))
                o = offs[gi] + (tq - qlo) * 128
                ops.append((tt, o, kt % 8, i == 0, i == len(kts) - 1))
        for i, (tt, o, slot, st_, sp_) in enumerate(ops):
            P.op("tensor", lambda t, o_=o, tt_=tt, slot_=slot, h_=h,
                 s0=st_, s1=sp_, ps_=ps:
                 t.matmul(ps_[:, tt_ * 65: tt_ * 65 + 65],
                          c.PT[h_ % 2][:, o_:o_ + 128],
                          c.VC[:, slot_ * 1040 + h_ * 65: slot_ * 1040 + (h_ + 1) * 65],
                          start=s0, stop=s1),
                 rd=ptok + ("vc",), wr=(btok,),
                 inc="eng" if i == len(ops) - 1 else None)

    def emit_stt(h):
        ps = (c.PS_A if h % 2 == 0 else c.PS_B)[:, 0:512]
        btok = "ps5" if h % 2 == 0 else "ps6"
        pv = ps[:, 0:260].rearrange("p (q w) -> p q w", w=65)
        rs = c.RS4[:, (h % 2) * 4:(h % 2) * 4 + 4]
        P.op("vector", lambda v, rs_=rs, pv_=pv: v.reciprocal(
            rs_.rearrange("p (q w) -> p q w", w=1), pv_[:, 0:4, 0:1]),
            rd=(btok,), wr=("rs4",))
        for tt in range(4):
            xk = f"x{tt}"
            P.op("vector", lambda v, tt_=tt, h_=h, ps_=ps, rs_=rs:
                 v.scalar_tensor_tensor(
                     c.XRES[:, tt_ * 1024 + h_ * 64: tt_ * 1024 + h_ * 64 + 64],
                     ps_[:, tt_ * 65 + 1: tt_ * 65 + 65],
                     rs_[:, tt_:tt_ + 1],
                     c.XRES[:, tt_ * 1024 + h_ * 64: tt_ * 1024 + h_ * 64 + 64],
                     ALU.mult, ALU.add),
                 rd=(btok, "rs4", xk), wr=(xk,), no_self_raw=(xk,))

    if h_from >= 2:
        emit_av(h_from - 2)
        emit_stt(h_from - 2)
    for h in range(h_from, h_to):
        emit_s(h)
        emit_exp(h)
        emit_masks(h)
        if h >= 1 and not scores_only and (h_from == 0 or h > h_from):
            emit_av(h - 1)
            emit_stt(h - 1)
        elif h >= 1 and not scores_only and h == h_from:
            emit_av(h - 1)
            emit_stt(h - 1)
    if h_to == H:
        emit_av(H - 1)
        emit_stt(H - 1)


def kernel(L0_attn_x, L0_attn_w_kqv, L0_ffn_W_r, L0_ffn_W_k, L0_ffn_W_v,
           L0_ffn_mu_r, L0_ffn_mu_k):
    ident, id8, m1, m2, biasj, biasr = _host_consts()
    nc = build(NS)
    base = {
        "wkqv": np.ascontiguousarray(np.asarray(L0_attn_w_kqv, np.float32)),
        "wr": np.ascontiguousarray(np.asarray(L0_ffn_W_r, np.float32)),
        "wk": np.ascontiguousarray(np.asarray(L0_ffn_W_k, np.float32)),
        "wv": np.ascontiguousarray(np.asarray(L0_ffn_W_v, np.float32)),
        "mur": np.ascontiguousarray(np.asarray(L0_ffn_mu_r, np.float32)),
        "muk": np.ascontiguousarray(np.asarray(L0_ffn_mu_k, np.float32)),
        "identc": ident, "id8c": id8, "m1c": m1, "m2c": m2,
        "biasjc": biasj, "biasrc": biasr,
    }
    x_full = np.asarray(L0_attn_x, np.float32)
    in_maps = []
    for b in range(8):
        m = dict(base)
        m["x"] = np.ascontiguousarray(x_full[b])
        in_maps.append(m)
    res = run_bass_kernel_spmd(nc, in_maps, core_ids=list(range(8)))
    out = np.stack([np.asarray(res.results[b]["out"], np.float32)
                    for b in range(8)], axis=0)
    return out


# version 16
# speedup vs baseline: 1.0217x; 1.0217x over previous
import contextlib
import numpy as np
import ml_dtypes
import concourse.bass as bass
import concourse.mybir as mybir
from concourse.bass_utils import run_bass_kernel_spmd

F32 = mybir.dt.float32
BF16 = mybir.dt.bfloat16
AF = mybir.ActivationFunctionType
ALU = mybir.AluOpType
AX = mybir.AxisListType

SEQ = 4096
D = 1024
H = 16
NT = 32
NS = 8
EPS = 1e-5
BATCHED_EXP_H0 = 3

SLOPES = [2.0 ** (-(h + 1) / 2.0) for h in range(H)]
ENGS = ("tensor", "scalar", "vector", "gpsimd", "sync")
ESEM = {"tensor": "pe", "scalar": "act", "vector": "dve", "gpsimd": "pool"}


def _host_consts():
    p = np.arange(128, dtype=np.float32)[:, None]
    ident = np.eye(128, dtype=np.float32).astype(ml_dtypes.bfloat16)
    id8 = np.eye(8, dtype=np.float32)
    j = np.arange(128, dtype=np.float32)[None, :]
    m1 = (p >= j).astype(np.float32).astype(ml_dtypes.bfloat16)
    m2 = (j >= p).astype(np.float32).astype(ml_dtypes.bfloat16)
    biasj = np.zeros((128, 80), np.float32)
    for h in range(H):
        for jj in range(5):
            biasj[:, h * 5 + jj] = SLOPES[h] * (p[:, 0] - 128.0 * jj - 64.0)
    biasr = np.zeros((128, 128), np.float32)
    for h in range(H):
        for r in range(-4, 4):
            biasr[:, h * 8 + r + 4] = SLOPES[h] * (p[:, 0] + 128.0 * r - 256.0)
    return ident, id8, m1, m2, biasj, biasr


class Prog:

    def __init__(self, nc, sems):
        self.nc = nc
        self.sem = sems
        self.cnt = {k: 0 for k in sems}
        self.q = {e: [] for e in ENGS}
        self.mark = {e: {k: 0 for k in sems} for e in ENGS}
        self.res = {}
        self.pending = {e: [] for e in ENGS}

    def _edge(self, waccum, eng, semkey, val, skip_same=False):
        if skip_same:
            return
        waccum.append((semkey, val))

    def op(self, eng, fn, rd=(), wr=(), inc="eng", step=1, waits=(),
           no_self_raw=()):
        if inc == "eng":
            semkey = ESEM[eng]
        elif inc is None:
            semkey = None
        else:
            semkey = inc
        w = list(waits)
        for t in rd:
            e = self.res.get(t)
            if e is None:
                continue
            assert e["w"] != "pending", f"read of pending-write token {t}"
            if e["w"] is not None:
                weng, wsem, wval = e["w"]
                same = (weng == eng and wsem in ESEM.values())
                if same and (weng == "tensor" or t in no_self_raw):
                    pass
                else:
                    w.append((wsem, wval))
        for t in wr:
            e = self.res.get(t)
            if e is None:
                continue
            if e["w"] is not None and e["w"] != "pending":
                weng, wsem, wval = e["w"]
                same_q = (weng == eng and (wsem in ESEM.values() or wsem == semkey))
                if not same_q:
                    w.append((wsem, wval))
            for (reng, rsem), rval in e["r"].items():
                same_q = (reng == eng and rsem in ESEM.values())
                if not same_q:
                    w.append((rsem, rval))
        ww = []
        for k, v in w:
            if v > self.mark[eng][k]:
                ww.append((k, v))
                self.mark[eng][k] = v
        tag = None
        if semkey is not None:
            self.cnt[semkey] += step
            tag = (semkey, step)
        self.q[eng].append((ww, fn, tag))
        val = self.cnt[semkey] if semkey is not None else None
        if semkey is None:
            for t in wr:
                self.pending[eng].append(t)
                self.res[t] = {"w": "pending", "r": {}}
        else:
            for t in list(self.pending[eng]) + list(wr):
                self.res[t] = {"w": (eng, semkey, val), "r": {}}
            self.pending[eng] = []
            for t in rd:
                e = self.res.setdefault(t, {"w": None, "r": {}})
                e["r"][(eng, semkey)] = val
        return val

    def flush(self):
        live = {e: lst for e, lst in self.q.items() if lst}
        if live:
            with self.nc.Block() as blk:
                for ename, lst in live.items():
                    def section(eng, lst=lst):
                        for w, fn, tag in lst:
                            for k, v in w:
                                eng.wait_ge(self.sem[k], v)
                            instr = fn(eng)
                            if tag is not None:
                                instr.then_inc(self.sem[tag[0]], tag[1])
                    getattr(blk, ename)(section)
        self.q = {e: [] for e in ENGS}
        for e in ENGS:
            for k in ("pe", "act", "dve", "pool"):
                self.mark[e][k] = max(self.mark[e][k], self.cnt[k])


class Ctx:
    pass


def sbanks(a, w):
    return tuple(f"ps{b}" for b in range(a // 512, (a + w - 1) // 512 + 1))


def build(ns=NS):
    nc = bass.Bass("TRN2", target_bir_lowering=False,
                   detect_race_conditions=False)

    dr = {}
    dr["x"] = nc.dram_tensor("x", [SEQ, D], F32, kind="ExternalInput")
    dr["wkqv"] = nc.dram_tensor("wkqv", [D, 3 * D], F32, kind="ExternalInput")
    dr["wr"] = nc.dram_tensor("wr", [D, D], F32, kind="ExternalInput")
    dr["wk"] = nc.dram_tensor("wk", [D, D], F32, kind="ExternalInput")
    dr["wv"] = nc.dram_tensor("wv", [D, D], F32, kind="ExternalInput")
    dr["mur"] = nc.dram_tensor("mur", [D], F32, kind="ExternalInput")
    dr["muk"] = nc.dram_tensor("muk", [D], F32, kind="ExternalInput")
    dr["identc"] = nc.dram_tensor("identc", [128, 128], BF16, kind="ExternalInput")
    dr["id8c"] = nc.dram_tensor("id8c", [8, 8], F32, kind="ExternalInput")
    dr["m1c"] = nc.dram_tensor("m1c", [128, 128], BF16, kind="ExternalInput")
    dr["m2c"] = nc.dram_tensor("m2c", [128, 128], BF16, kind="ExternalInput")
    dr["biasjc"] = nc.dram_tensor("biasjc", [128, 80], F32, kind="ExternalInput")
    dr["biasrc"] = nc.dram_tensor("biasrc", [128, 128], F32, kind="ExternalInput")
    dr["out"] = nc.dram_tensor("out", [SEQ, D], F32, kind="ExternalOutput")

    c = Ctx()
    c.nc = nc
    c.dr = dr

    with contextlib.ExitStack() as _es:
        def sb(name, shape, dtype):
            return _es.enter_context(nc.sbuf_tensor(name, shape, dtype))
        c.WKQV = sb("WKQV", [128, 8 * 3072], BF16)
        c.WTR = sb("WTR", [128, 8 * 1024], BF16)
        c.WTK = sb("WTK", [128, 8 * 1024], BF16)
        c.WTV = sb("WTV", [128, 8 * 1024], BF16)
        c.KC = sb("KC", [128, 8 * 1024], BF16)
        c.VC = sb("VC", [128, 8 * 1040], BF16)
        c.QB = sb("QB", [128, 8 * 512], BF16)
        c.XRES = sb("XRES", [128, 4 * 1024], F32)
        c.LNKA = sb("LNKA", [128, 8 * 513], BF16)
        c.XRT = sb("XRT", [128, 8 * 512], BF16)
        c.XKT = sb("XKT", [128, 8 * 512], BF16)
        c.RBUF = sb("RBUF", [128, 1024], BF16)
        c.LNB = sb("LNB", [128, 2 * 1024], BF16)
        c.SCR = sb("SCR", [128, 7168], BF16)
        c.EWS = sb("EWS", [128, 3584], BF16)
        c.MU = sb("MU", [128, 16], F32)
        c.IDENT = sb("IDENT", [128, 128], BF16)
        c.ID8 = sb("ID8", [8, 8], F32)
        c.M1 = sb("M1", [128, 128], BF16)
        c.M2 = sb("M2", [128, 128], BF16)
        c.BIASJ = sb("BIASJ", [128, 80], F32)
        c.BIASR = sb("BIASR", [128, 128], F32)
        c.EPSC = sb("EPSC", [128, 1], F32)
        c.STT = sb("STT", [128, 48], F32)
        c.RS4 = sb("RS4", [128, 8], F32)
        c.SAVE1 = sb("SAVE1", [128, 8], BF16)
        c.SAVE3 = sb("SAVE3", [128, 8], BF16)
        c.PS_S = _es.enter_context(nc.psum_tensor("PS_S", [128, 2560], F32))
        c.PS_A = _es.enter_context(nc.psum_tensor("PS_A", [128, 512], F32))
        c.PS_B = _es.enter_context(nc.psum_tensor("PS_B", [128, 512], F32))
        c.PS_X = _es.enter_context(nc.psum_tensor("PS_X", [128, 512], F32))
        sems = {}
        for sname in ("scd", "sxd0", "sxd1", "sxd2", "sxd3", "sstg0", "sstg1",
                      "po0", "po1", "po2", "po3", "pe", "act", "dve", "pool"):
            sems[sname] = _es.enter_context(nc.semaphore("s_" + sname))
        c.P = Prog(nc, sems)

        c.MUS = c.SCR[0:8, 0:512].bitcast(F32)
        c.LNT = c.LNKA[:].rearrange("p (c w) -> p c w", w=513)
        c.PT = [c.SCR[:, 0:2560], c.SCR[:, 2560:5120]]
        c.PTtok = [("scrA", "scrB"), ("scrC", "scrD")]
        c.STG = [c.SCR[:, 0:2048].bitcast(F32), c.SCR[:, 2048:4096].bitcast(F32)]
        c.STGtok = [("scrA",), ("scrB", "scrC")]
        c.WSTG = c.SCR[:, 4096:5120]
        c.SQD = c.SCR[:, 5120:7168].bitcast(F32)
        c.EW_D = c.EWS[:, 0:512]
        c.KSC = [c.EWS[:, 512:1536].bitcast(F32), c.EWS[:, 1536:2560].bitcast(F32)]
        c.KSCtok = ["ewb", "ewc"]
        c.TH = c.EWS[:, 512:1536]
        c.FV = [c.EWS[:, 1536:2560], c.EWS[:, 2560:3584]]
        c.FVtok = ["ewc", "ewf"]

        build_program(c, ns)
    return nc


def build_program(c, ns):
    P = c.P
    preamble(c)
    phase_ln(c, 0, 1)
    weights(c)
    for s in range(ns):
        if s > 0:
            phase_ln(c, s, 1)
        phase_mix_k(c, s)
        phase_rv(c, s, store=False)
        phase_ln(c, s, 2)
        phase_kqv(c, s)
        phase_attn(c, s)
        phase_ln(c, s, 3)
        phase_mix_k(c, s)
        phase_rv(c, s, store=True, refill=(s + 1 < ns))
    P.op("gpsimd", lambda g: g.engine_nop(),
         waits=[(f"po{tt}", P.cnt[f"po{tt}"]) for tt in range(4)], inc=None)
    P.flush()


def cdma(P, eng, dst, src, sem, rd=(), wr=()):
    return P.op(eng, lambda e, d=dst, s=src: e.dma_start(d, s), inc=sem,
                step=16, rd=rd, wr=wr)


def preamble(c):
    P, dr = c.P, c.dr
    for nm, tok in (("identc", "ident"), ("id8c", "id8"), ("m1c", "m1"),
                    ("m2c", "m2"), ("biasjc", "biasj"), ("biasrc", "biasr")):
        dst = {"ident": c.IDENT, "id8": c.ID8, "m1": c.M1, "m2": c.M2,
               "biasj": c.BIASJ, "biasr": c.BIASR}[tok]
        cdma(P, "sync", dst[:], dr[nm].ap(), "scd", wr=(tok,))
    cdma(P, "sync", c.MUS[0:8, 0:128],
         dr["mur"].ap().rearrange("(c p) -> c p", p=128), "scd", wr=("scrA",))
    cdma(P, "sync", c.MUS[0:8, 128:256],
         dr["muk"].ap().rearrange("(c p) -> c p", p=128), "scd", wr=("scrA",))
    for tt in range(4):
        cdma(P, "sync", c.XRES[:, tt * 1024:(tt + 1) * 1024],
             dr["x"].ap()[tt * 128:(tt + 1) * 128, :], f"sxd{tt}", wr=(f"x{tt}",))
    P.op("vector", lambda v: v.memset(c.EPSC[:], EPS), wr=("epsc",))
    vco = c.VC[:].rearrange("p (g w) -> p g w", w=65)
    P.op("vector", lambda v: v.memset(vco[:, :, 0:1], 1.0), wr=("vc",))
    P.op("vector", lambda v: v.memset(c.LNT[:, :, 0:1], 0.0), wr=("lnt",))
    P.op("vector", lambda v: v.memset(c.SAVE1[:], 0.0), wr=("save1",))
    P.op("vector", lambda v: v.memset(c.SAVE3[:], 0.0), wr=("save3",))
    P.op("tensor", lambda t: t.matmul(
        c.PS_A[:, 0:8], c.MUS[0:8, 0:128], c.ID8[:], start=True, stop=True),
        rd=("scrA", "id8"), wr=("ps5",))
    P.op("tensor", lambda t: t.matmul(
        c.PS_A[:, 8:16], c.MUS[0:8, 128:256], c.ID8[:], start=True, stop=True),
        rd=("scrA", "id8"), wr=("ps5",))
    P.op("vector", lambda v: v.tensor_copy(c.MU[:], c.PS_A[:, 0:16]),
         rd=("ps5",), wr=("mu",))


def weights(c):
    P, dr = c.P, c.dr
    wd = {0: dr["wr"], 1: dr["wk"], 2: dr["wv"]}
    wt = {0: (c.WTR, "wtr"), 1: (c.WTK, "wtk"), 2: (c.WTV, "wtv")}
    jobs = ([("ffn", w, dj) for w in (1, 0, 2) for dj in range(8)] +
            [("kqv", cc, t3) for cc in range(8) for t3 in range(3)])
    m = 0
    for n, job in enumerate(jobs):
        kind = job[0]
        stok = c.STGtok[n % 2]
        if kind == "kqv":
            _, cc, t3 = job
            src = dr["wkqv"].ap()[cc * 128:(cc + 1) * 128, t3 * 1024:(t3 + 1) * 1024]
        else:
            _, w, dj = job
            src = wd[w].ap()[dj * 128:(dj + 1) * 128, :]
        cdma(P, "sync", c.STG[n % 2][:], src, f"sstg{n % 2}", wr=stok)
        if kind == "kqv":
            _, cc, t3 = job
            dst = c.WKQV[:, cc * 3072 + t3 * 1024: cc * 3072 + (t3 + 1) * 1024]
            if t3 == 1:
                P.op("vector", lambda v, d_=dst, s_=c.STG[n % 2]:
                     v.tensor_scalar_mul(d_, s_[:], 1.0 / 32.0),
                     rd=stok, wr=("wkqv",))
            else:
                P.op("vector", lambda v, d_=dst, s_=c.STG[n % 2]:
                     v.tensor_copy(d_, s_[:]), rd=stok, wr=("wkqv",))
        else:
            _, w, dj = job
            if w == 2:
                P.op("vector", lambda v, s_=c.STG[n % 2]:
                     v.tensor_scalar_mul(c.WSTG[:], s_[:], 0.5),
                     rd=stok, wr=("scrD",))
            else:
                P.op("vector", lambda v, s_=c.STG[n % 2]:
                     v.tensor_copy(c.WSTG[:], s_[:]), rd=stok, wr=("scrD",))
            ps = c.PS_S[:, (m % 2) * 1024:(m % 2 + 1) * 1024]
            ptoks = sbanks((m % 2) * 1024, 1024)
            for ci in range(8):
                P.op("tensor", lambda t, ps_=ps, ci_=ci:
                     t.matmul(ps_[:, ci_ * 128:(ci_ + 1) * 128],
                              c.WSTG[:, ci_ * 128:(ci_ + 1) * 128],
                              c.IDENT[:], start=True, stop=True),
                     rd=("scrD", "ident"), wr=ptoks,
                     inc="eng" if ci == 7 else None)
            src2 = ps.rearrange("p (ci w2) -> p ci w2", w2=128)
            dst2 = wt[w][0][:].rearrange("p (ci w2) -> p ci w2", w2=1024)[:, :, dj * 128:(dj + 1) * 128]
            P.op("scalar", lambda a, d_=dst2, s_=src2: a.copy(d_, s_),
                 rd=ptoks, wr=(wt[w][1],))
            m += 1
    P.flush()


def ln_tile_stats(c, tt):
    P = c.P
    ST = c.STT
    xk = f"x{tt}"
    xres = c.XRES[:, tt * 1024:(tt + 1) * 1024]
    P.op("scalar", lambda a, x_=xres, tt_=tt: a.activation(
        c.SQD[:], x_, AF.Square, accum_out=ST[:, 4 + tt_:5 + tt_]),
        rd=(xk,), wr=("scrE", f"sttS{tt}"))
    P.op("vector", lambda v, x_=xres, tt_=tt: v.reduce_sum(
        ST[:, tt_:tt_ + 1], x_, axis=AX.X), rd=(xk,), wr=(f"sttR{tt}",))


def phase_ln(c, s, which, skip_stats=False):
    P = c.P
    save, stok = {1: (c.SAVE1, "save1"), 2: (None, None),
                  3: (c.SAVE3, "save3")}[which]
    if which != 2 and s > 0:
        P.op("vector", lambda v, sv=save: v.tensor_copy(
            c.LNT[:, :, 0:1], sv[:].rearrange("p (cc w) -> p cc w", w=1)),
            rd=(stok,), wr=("lnt",))
    ST = c.STT
    if not skip_stats:
        for tt in range(4):
            ln_tile_stats(c, tt)
    sall = ("sttS0", "sttS1", "sttS2", "sttS3",
            "sttR0", "sttR1", "sttR2", "sttR3")
    P.op("vector", lambda v: v.scalar_tensor_tensor(
        ST[:, 8:12], ST[:, 0:4], -1.0 / (D * D), ST[:, 0:4],
        ALU.mult, ALU.mult), rd=sall, wr=("sttF",))
    P.op("vector", lambda v: v.scalar_tensor_tensor(
        ST[:, 12:16], ST[:, 4:8], 1.0 / D, ST[:, 8:12],
        ALU.mult, ALU.add), rd=sall + ("sttF",), wr=("sttF",))
    P.op("scalar", lambda a: a.activation(
        ST[:, 16:20], ST[:, 12:16], AF.Sqrt, bias=c.EPSC[:]),
        rd=("sttF", "epsc"), wr=("sttF",))
    P.op("vector", lambda v: v.reciprocal(
        ST[:, 20:24], ST[:, 16:20]), rd=("sttF",), wr=("sttF",))
    P.op("vector", lambda v: v.scalar_tensor_tensor(
        ST[:, 24:28], ST[:, 0:4], -1.0 / D, ST[:, 20:24],
        ALU.mult, ALU.mult), rd=sall + ("sttF",), wr=("sttF",))
    for tt in range(4):
        xk = f"x{tt}"
        xres = c.XRES[:, tt * 1024:(tt + 1) * 1024]
        ln = c.LNB[:, (tt % 2) * 1024:(tt % 2 + 1) * 1024]
        lt = f"lnb{tt % 2}"
        P.op("scalar", lambda a, x_=xres, tt_=tt, ln_=ln: a.activation(
            ln_, x_, AF.Identity, bias=ST[:, 24 + tt_:25 + tt_],
            scale=ST[:, 20 + tt_:21 + tt_]),
            rd=(xk, "sttF"), wr=(lt,))
        ps = c.PS_S[:, (tt % 2) * 1024:(tt % 2 + 1) * 1024]
        ptoks = sbanks((tt % 2) * 1024, 1024)
        for ci in range(8):
            P.op("tensor", lambda t, ps_=ps, ln_=ln, ci_=ci:
                 t.matmul(ps_[:, ci_ * 128:(ci_ + 1) * 128],
                          ln_[:, ci_ * 128:(ci_ + 1) * 128],
                          c.IDENT[:], start=True, stop=True),
                 rd=(lt, "ident"), wr=ptoks, inc="eng" if ci == 7 else None)
        src = ps.rearrange("p (ci w) -> p ci w", w=128)
        dst = c.LNT[:, :, 1 + tt * 128: 1 + (tt + 1) * 128]
        P.op("scalar", lambda a, d_=dst, s_=src: a.copy(d_, s_),
             rd=ptoks, wr=("lnt",))
    if save is not None:
        P.op("vector", lambda v, sv=save: v.tensor_copy(
            sv[:].rearrange("p (cc w) -> p cc w", w=1), c.LNT[:, :, 512:513]),
            rd=("lnt",), wr=(stok,))


def phase_mix_k(c, s):
    P = c.P
    d3 = c.XKT[:].rearrange("p (cc w) -> p cc w", w=512)
    xtoks = tuple(f"xkt{cc}" for cc in range(8))
    P.op("vector", lambda v: v.tensor_sub(
        d3, c.LNT[:, :, 1:513], c.LNT[:, :, 0:512]), rd=("lnt",), wr=xtoks)
    for cc in range(8):
        sh = c.LNT[:, cc, 0:512]
        P.op("vector", lambda v, sh_=sh, cc_=cc: v.scalar_tensor_tensor(
            c.XRT[:, cc_ * 512:(cc_ + 1) * 512],
            c.XKT[:, cc_ * 512:(cc_ + 1) * 512],
            c.MU[:, cc_:cc_ + 1], sh_, ALU.mult, ALU.add),
            rd=(f"xkt{cc}", "lnt", "mu"), wr=("xrt",))
        P.op("vector", lambda v, sh_=sh, cc_=cc: v.scalar_tensor_tensor(
            c.XKT[:, cc_ * 512:(cc_ + 1) * 512],
            c.XKT[:, cc_ * 512:(cc_ + 1) * 512],
            c.MU[:, 8 + cc_:9 + cc_], sh_, ALU.mult, ALU.add),
            rd=(f"xkt{cc}", "lnt", "mu"), wr=(f"xkt{cc}",))
    for dp in range(8):
        ps = (c.PS_A if dp % 2 == 0 else c.PS_B)[:, 0:512]
        btok = "ps5" if dp % 2 == 0 else "ps6"
        for cc in range(8):
            rdt = (("wtk",) + tuple(f"xkt{i}" for i in range(8))) if cc == 7 \
                else ("wtk", f"xkt{cc}")
            P.op("tensor", lambda t, ps_=ps, cc_=cc, dp_=dp: t.matmul(
                ps_, c.WTK[:, cc_ * 1024 + dp_ * 128: cc_ * 1024 + (dp_ + 1) * 128],
                c.XKT[:, cc_ * 512:(cc_ + 1) * 512],
                start=(cc_ == 0), stop=(cc_ == 7)),
                rd=rdt, wr=(btok,), inc="eng" if cc == 7 else None)
        kt = c.KSCtok[dp % 2]
        P.op("scalar", lambda a, ps_=ps, dp_=dp: a.activation(
            c.KSC[dp_ % 2][:, 0:512], ps_, AF.Relu), rd=(btok,), wr=(kt,))
        P.op("vector", lambda v, dp_=dp: v.tensor_mul(
            c.LNT[:, dp_, 1:513], c.KSC[dp_ % 2][:, 0:512],
            c.KSC[dp_ % 2][:, 0:512]), rd=(kt,), wr=("lnt",))


def phase_rv(c, s, store, refill=False, post_tile=None):
    P, dr = c.P, c.dr

    def regions(tt):
        if tt % 2 == 0:
            return ((c.PS_S[:, 0:512], "ps0"), (c.PS_S[:, 512:1024], "ps1"),
                    (c.PS_S[:, 1024:1536], "ps2"), (c.PS_S[:, 1536:2048], "ps3"))
        return ((c.PS_A[:, 0:512], "ps5"), (c.PS_B[:, 0:512], "ps6"),
                (c.PS_S[:, 2048:2560], "ps4"), (c.PS_X[:, 0:512], "ps7"))

    for tt in range(4):
        (r0, r0t), (r1, r1t), (v0, v0t), (v1, v1t) = regions(tt)
        for n, (ps, pt) in enumerate(((r0, r0t), (r1, r1t))):
            for cc in range(8):
                P.op("tensor", lambda t, ps_=ps, cc_=cc, n_=n, tt_=tt:
                     t.matmul(ps_,
                              c.XRT[:, cc_ * 512 + tt_ * 128: cc_ * 512 + tt_ * 128 + 128],
                              c.WTR[:, cc_ * 1024 + n_ * 512: cc_ * 1024 + (n_ + 1) * 512],
                              start=(cc_ == 0), stop=(cc_ == 7)),
                     rd=("xrt", "wtr"), wr=(pt,), inc="eng" if cc == 7 else None)
        for n, (ps, pt) in enumerate(((v0, v0t), (v1, v1t))):
            for cc in range(8):
                P.op("tensor", lambda t, ps_=ps, cc_=cc, n_=n, tt_=tt:
                     t.matmul(ps_,
                              c.LNT[:, cc_, 1 + tt_ * 128: 1 + tt_ * 128 + 128],
                              c.WTV[:, cc_ * 1024 + n_ * 512: cc_ * 1024 + (n_ + 1) * 512],
                              start=(cc_ == 0), stop=(cc_ == 7)),
                     rd=("lnt", "wtv"), wr=(pt,), inc="eng" if cc == 7 else None)
        P.op("scalar", lambda a, r0_=r0: a.activation(
            c.TH[:, 0:512], r0_, AF.Tanh, scale=0.5), rd=(r0t,), wr=("ewb",))
        P.op("scalar", lambda a, r1_=r1: a.activation(
            c.TH[:, 512:1024], r1_, AF.Tanh, scale=0.5), rd=(r1t,), wr=("ewb",))
        P.op("vector", lambda v: v.tensor_scalar(
            c.RBUF[:], c.TH[:], 0.5, 0.5, ALU.mult, ALU.add),
            rd=("ewb",), wr=("rbuf",))
        fv = c.FV[tt % 2]
        ft = c.FVtok[tt % 2]
        P.op("vector", lambda v, v0_=v0, fv_=fv: v.tensor_mul(
            fv_[:, 0:512], c.RBUF[:, 0:512], v0_), rd=("rbuf", v0t), wr=(ft,))
        P.op("vector", lambda v, v1_=v1, fv_=fv: v.tensor_mul(
            fv_[:, 512:1024], c.RBUF[:, 512:1024], v1_), rd=("rbuf", v1t), wr=(ft,))
        xres = c.XRES[:, tt * 1024:(tt + 1) * 1024]
        xk = f"x{tt}"
        P.op("vector", lambda g, x_=xres, fv_=fv: g.tensor_add(
            x_, x_, fv_[:]), rd=(xk, ft), wr=(xk,))
        if store:
            T = 4 * s + tt
            P.op("gpsimd", lambda g, x_=xres, T_=T: g.dma_start(
                dr["out"].ap()[T_ * 128:(T_ + 1) * 128, :], x_),
                inc=f"po{tt}", step=16, rd=(xk,))
            if refill:
                Tn = T + 4
                P.op("sync", lambda e, x_=xres, Tn_=Tn: e.dma_start(
                    x_, dr["x"].ap()[Tn_ * 128:(Tn_ + 1) * 128, :]),
                    inc=f"sxd{tt}", step=16, wr=(xk,))
        if post_tile is not None:
            post_tile(tt)


def phase_kqv(c, s):
    P = c.P
    for part in range(2):
        for j in range(8):
            a0 = (part * 2 + j % 2) * 512
            ps = c.PS_S[:, a0:a0 + 512]
            btok = sbanks(a0, 512)
            for cc in range(8):
                P.op("tensor", lambda t, ps_=ps, cc_=cc, j_=j, p_=part:
                     t.matmul(ps_,
                              c.WKQV[:, cc_ * 3072 + p_ * 1024 + j_ * 128:
                                     cc_ * 3072 + p_ * 1024 + (j_ + 1) * 128],
                              c.LNT[:, cc_, 1:513],
                              start=(cc_ == 0), stop=(cc_ == 7)),
                     rd=("wkqv", "lnt"), wr=btok, inc="eng" if cc == 7 else None)
            if part == 0:
                dst = c.KC[:, j * 1024 + (s % 2) * 512: j * 1024 + (s % 2) * 512 + 512]
                dtok = "kc"
            else:
                dst = c.QB[:, j * 512:(j + 1) * 512]
                dtok = "qb"
            P.op("scalar", lambda a, d_=dst, ps_=ps: a.copy(d_, ps_),
                 rd=btok, wr=(dtok,))
    for tt in range(4):
        slot = (4 * s + tt) % 8
        pss = (((c.PS_S[:, 2048:2560], "ps4"), (c.PS_X[:, 0:512], "ps7"))
               if tt % 2 == 0 else
               ((c.PS_A[:, 0:512], "ps5"), (c.PS_B[:, 0:512], "ps6")))
        vcs = c.VC[:, slot * 1040:(slot + 1) * 1040].rearrange("p (h w) -> p h w", w=65)
        for n, (ps, pt) in enumerate(pss):
            for cc in range(8):
                P.op("tensor", lambda t, ps_=ps, cc_=cc, n_=n, tt_=tt:
                     t.matmul(ps_,
                              c.LNT[:, cc_, 1 + tt_ * 128: 1 + tt_ * 128 + 128],
                              c.WKQV[:, cc_ * 3072 + 2048 + n_ * 512:
                                     cc_ * 3072 + 2048 + (n_ + 1) * 512],
                              start=(cc_ == 0), stop=(cc_ == 7)),
                     rd=("lnt", "wkqv"), wr=(pt,), inc="eng" if cc == 7 else None)
            src = ps.rearrange("p (h w) -> p h w", w=64)
            P.op("vector", lambda v, d_=vcs[:, 8 * n:8 * n + 8, 1:65], s_=src:
                 v.tensor_copy(d_, s_), rd=(pt,), wr=("vc",))


def attn_schedule(s):
    groups = []
    for kt in range(4 * s - 4, 4 * s + 4):
        if kt < 0:
            continue
        qlo = max(kt, 4 * s)
        qhi = min(kt + 4, 4 * s + 3)
        groups.append((kt, qlo, qhi))
    offs = []
    o = 0
    for (kt, qlo, qhi) in groups:
        offs.append(o)
        o += (qhi - qlo + 1) * 128
    return groups, offs


def phase_attn(c, s):
    P = c.P
    groups, offs = attn_schedule(s)

    def emit_s(h):
        j = h >> 1
        half = (h & 1) * 64
        pieces = []
        for gi, (kt, qlo, qhi) in enumerate(groups):
            w = (qhi - qlo + 1) * 128
            a = offs[gi]
            while w > 0:
                wp = min(w, 512 - a % 512)
                pieces.append((kt, qlo, a, a - offs[gi], wp))
                a += wp
                w -= wp
        for pi, (kt, qlo, a, rel, wp) in enumerate(pieces):
            slot = kt % 8
            P.op("tensor", lambda t, a_=a, rel_=rel, wp_=wp, qlo_=qlo, slot_=slot:
                 t.matmul(c.PS_S[:, a_:a_ + wp_],
                          c.KC[half:half + 64,
                               j * 1024 + slot_ * 128: j * 1024 + (slot_ + 1) * 128],
                          c.QB[half:half + 64,
                               j * 512 + (qlo_ - 4 * s) * 128 + rel_:
                               j * 512 + (qlo_ - 4 * s) * 128 + rel_ + wp_],
                          start=True, stop=True, tile_position=(half, 0)),
                 rd=("kc", "qb"), wr=sbanks(a, wp),
                 inc="eng" if pi == len(pieces) - 1 else None)

    def emit_exp(h):
        pt = c.PT[h % 2]
        ptok = c.PTtok[h % 2]
        for gi, (kt, qlo, qhi) in enumerate(groups):
            if h < BATCHED_EXP_H0:
                for tt in range(qlo - 4 * s, qhi - 4 * s + 1):
                    jj = (4 * s + tt) - kt
                    o = offs[gi] + (tt - (qlo - 4 * s)) * 128
                    P.op("scalar", lambda a, o_=o, h_=h, jj_=jj: a.activation(
                        c.PT[h_ % 2][:, o_:o_ + 128], c.PS_S[:, o_:o_ + 128],
                        AF.Exp, bias=c.BIASJ[:, h_ * 5 + jj_: h_ * 5 + jj_ + 1]),
                        rd=sbanks(o, 128) + ("biasj",), wr=ptok)
            else:
                r = kt - 4 * s + 4
                w = (qhi - qlo + 1) * 128
                P.op("scalar", lambda a, gi_=gi, w_=w, h_=h, r_=r: a.activation(
                    c.PT[h_ % 2][:, offs[gi_]:offs[gi_] + w_],
                    c.PS_S[:, offs[gi_]:offs[gi_] + w_], AF.Exp,
                    bias=c.BIASR[:, h_ * 8 + r_: h_ * 8 + r_ + 1]),
                    rd=sbanks(offs[gi], w) + ("biasr",), wr=ptok)

    def emit_masks(h):
        ptok = c.PTtok[h % 2]
        for gi, (kt, qlo, qhi) in enumerate(groups):
            if kt < 4 * s and kt + 4 <= 4 * s + 3:
                o = offs[gi] + (kt + 4 - qlo) * 128
                P.op("vector", lambda g, o_=o, h_=h: g.tensor_mul(
                    c.PT[h_ % 2][:, o_:o_ + 128], c.PT[h_ % 2][:, o_:o_ + 128],
                    c.M1[:]), rd=ptok + ("m1",), wr=ptok)
            if kt >= 4 * s:
                o = offs[gi]
                P.op("vector", lambda g, o_=o, h_=h: g.tensor_mul(
                    c.PT[h_ % 2][:, o_:o_ + 128], c.PT[h_ % 2][:, o_:o_ + 128],
                    c.M2[:]), rd=ptok + ("m2",), wr=ptok)

    def emit_av(h):
        ps = c.PS_A if h % 2 == 0 else c.PS_B
        btok = "ps5" if h % 2 == 0 else "ps6"
        ptok = c.PTtok[h % 2]
        ops = []
        for tt in range(4):
            tq = 4 * s + tt
            kts = [g for g in groups if g[1] <= tq <= g[2]]
            for i, (kt, qlo, qhi) in enumerate(kts):
                gi = groups.index((kt, qlo, qhi))
                o = offs[gi] + (tq - qlo) * 128
                ops.append((tt, o, kt % 8, i == 0, i == len(kts) - 1))
        for i, (tt, o, slot, st_, sp_) in enumerate(ops):
            P.op("tensor", lambda t, o_=o, tt_=tt, slot_=slot, h_=h,
                 s0=st_, s1=sp_, ps_=ps:
                 t.matmul(ps_[:, tt_ * 65: tt_ * 65 + 65],
                          c.PT[h_ % 2][:, o_:o_ + 128],
                          c.VC[:, slot_ * 1040 + h_ * 65: slot_ * 1040 + (h_ + 1) * 65],
                          start=s0, stop=s1),
                 rd=ptok + ("vc",), wr=(btok,),
                 inc="eng" if i == len(ops) - 1 else None)

    def emit_stt(h):
        ps = (c.PS_A if h % 2 == 0 else c.PS_B)[:, 0:512]
        btok = "ps5" if h % 2 == 0 else "ps6"
        pv = ps[:, 0:260].rearrange("p (q w) -> p q w", w=65)
        rs = c.RS4[:, (h % 2) * 4:(h % 2) * 4 + 4]
        P.op("vector", lambda v, rs_=rs, pv_=pv: v.reciprocal(
            rs_.rearrange("p (q w) -> p q w", w=1), pv_[:, 0:4, 0:1]),
            rd=(btok,), wr=("rs4",))
        for tt in range(4):
            xk = f"x{tt}"
            P.op("vector", lambda v, tt_=tt, h_=h, ps_=ps, rs_=rs:
                 v.scalar_tensor_tensor(
                     c.XRES[:, tt_ * 1024 + h_ * 64: tt_ * 1024 + h_ * 64 + 64],
                     ps_[:, tt_ * 65 + 1: tt_ * 65 + 65],
                     rs_[:, tt_:tt_ + 1],
                     c.XRES[:, tt_ * 1024 + h_ * 64: tt_ * 1024 + h_ * 64 + 64],
                     ALU.mult, ALU.add),
                 rd=(btok, "rs4", xk), wr=(xk,), no_self_raw=(xk,))

    for h in range(H):
        emit_s(h)
        emit_exp(h)
        emit_masks(h)
        if h >= 1:
            emit_av(h - 1)
            emit_stt(h - 1)
    emit_av(H - 1)
    emit_stt(H - 1)


def kernel(L0_attn_x, L0_attn_w_kqv, L0_ffn_W_r, L0_ffn_W_k, L0_ffn_W_v,
           L0_ffn_mu_r, L0_ffn_mu_k):
    ident, id8, m1, m2, biasj, biasr = _host_consts()
    nc = build(NS)
    base = {
        "wkqv": np.ascontiguousarray(np.asarray(L0_attn_w_kqv, np.float32)),
        "wr": np.ascontiguousarray(np.asarray(L0_ffn_W_r, np.float32)),
        "wk": np.ascontiguousarray(np.asarray(L0_ffn_W_k, np.float32)),
        "wv": np.ascontiguousarray(np.asarray(L0_ffn_W_v, np.float32)),
        "mur": np.ascontiguousarray(np.asarray(L0_ffn_mu_r, np.float32)),
        "muk": np.ascontiguousarray(np.asarray(L0_ffn_mu_k, np.float32)),
        "identc": ident, "id8c": id8, "m1c": m1, "m2c": m2,
        "biasjc": biasj, "biasrc": biasr,
    }
    x_full = np.asarray(L0_attn_x, np.float32)
    in_maps = []
    for b in range(8):
        m = dict(base)
        m["x"] = np.ascontiguousarray(x_full[b])
        in_maps.append(m)
    res = run_bass_kernel_spmd(nc, in_maps, core_ids=list(range(8)))
    out = np.stack([np.asarray(res.results[b]["out"], np.float32)
                    for b in range(8)], axis=0)
    return out


# version 18
# speedup vs baseline: 1.2341x; 1.2079x over previous
import contextlib
import numpy as np
import ml_dtypes
import concourse.bass as bass
import concourse.mybir as mybir
from concourse.bass_utils import run_bass_kernel_spmd

F32 = mybir.dt.float32
BF16 = mybir.dt.bfloat16
AF = mybir.ActivationFunctionType
ALU = mybir.AluOpType
AX = mybir.AxisListType

SEQ = 4096
D = 1024
H = 16
NT = 32
NS = 8
EPS = 1e-5
BATCHED_EXP_H0 = 3

SLOPES = [2.0 ** (-(h + 1) / 2.0) for h in range(H)]
ENGS = ("tensor", "scalar", "vector", "gpsimd", "sync")
ESEM = {"tensor": "pe", "scalar": "act", "vector": "dve", "gpsimd": "pool"}


def _host_consts():
    p = np.arange(128, dtype=np.float32)[:, None]
    ident = np.eye(128, dtype=np.float32).astype(ml_dtypes.bfloat16)
    id8 = np.eye(8, dtype=np.float32)
    j = np.arange(128, dtype=np.float32)[None, :]
    m1 = (p >= j).astype(np.float32).astype(ml_dtypes.bfloat16)
    m2 = (j >= p).astype(np.float32).astype(ml_dtypes.bfloat16)
    biasj = np.zeros((128, 80), np.float32)
    for h in range(H):
        for jj in range(5):
            biasj[:, h * 5 + jj] = SLOPES[h] * (p[:, 0] - 128.0 * jj - 64.0)
    biasr = np.zeros((128, 128), np.float32)
    for h in range(H):
        for r in range(-4, 4):
            biasr[:, h * 8 + r + 4] = SLOPES[h] * (p[:, 0] + 128.0 * r - 256.0)
    return ident, id8, m1, m2, biasj, biasr


class Prog:

    def __init__(self, nc, sems):
        self.nc = nc
        self.sem = sems
        self.cnt = {k: 0 for k in sems}
        self.q = {e: [] for e in ENGS}
        self.mark = {e: {k: 0 for k in sems} for e in ENGS}
        self.res = {}
        self.pending = {e: [] for e in ENGS}

    def _edge(self, waccum, eng, semkey, val, skip_same=False):
        if skip_same:
            return
        waccum.append((semkey, val))

    def op(self, eng, fn, rd=(), wr=(), inc="eng", step=1, waits=(),
           no_self_raw=()):
        if inc == "eng":
            semkey = ESEM[eng]
        elif inc is None:
            semkey = None
        else:
            semkey = inc
        w = list(waits)
        for t in rd:
            e = self.res.get(t)
            if e is None:
                continue
            assert e["w"] != "pending", f"read of pending-write token {t}"
            if e["w"] is not None:
                weng, wsem, wval = e["w"]
                same = (weng == eng and wsem in ESEM.values())
                if same and (weng == "tensor" or t in no_self_raw):
                    pass
                else:
                    w.append((wsem, wval))
        for t in wr:
            e = self.res.get(t)
            if e is None:
                continue
            if e["w"] is not None and e["w"] != "pending":
                weng, wsem, wval = e["w"]
                same_q = (weng == eng and (wsem in ESEM.values() or wsem == semkey))
                if not same_q:
                    w.append((wsem, wval))
            for (reng, rsem), rval in e["r"].items():
                same_q = (reng == eng and rsem in ESEM.values())
                if not same_q:
                    w.append((rsem, rval))
        ww = []
        for k, v in w:
            if v > self.mark[eng][k]:
                ww.append((k, v))
                self.mark[eng][k] = v
        tag = None
        if semkey is not None:
            self.cnt[semkey] += step
            tag = (semkey, step)
        self.q[eng].append((ww, fn, tag))
        val = self.cnt[semkey] if semkey is not None else None
        if semkey is None:
            for t in wr:
                self.pending[eng].append(t)
                self.res[t] = {"w": "pending", "r": {}}
        else:
            for t in list(self.pending[eng]) + list(wr):
                self.res[t] = {"w": (eng, semkey, val), "r": {}}
            self.pending[eng] = []
            for t in rd:
                e = self.res.setdefault(t, {"w": None, "r": {}})
                e["r"][(eng, semkey)] = val
        return val

    def flush(self):
        live = {e: lst for e, lst in self.q.items() if lst}
        if live:
            with self.nc.Block() as blk:
                for ename, lst in live.items():
                    def section(eng, lst=lst):
                        for w, fn, tag in lst:
                            for k, v in w:
                                eng.wait_ge(self.sem[k], v)
                            instr = fn(eng)
                            if tag is not None:
                                instr.then_inc(self.sem[tag[0]], tag[1])
                    getattr(blk, ename)(section)
        self.q = {e: [] for e in ENGS}
        for e in ENGS:
            for k in ("pe", "act", "dve", "pool"):
                self.mark[e][k] = max(self.mark[e][k], self.cnt[k])


class Ctx:
    pass


def sbanks(a, w):
    return tuple(f"ps{b}" for b in range(a // 512, (a + w - 1) // 512 + 1))


def build(ns=NS):
    nc = bass.Bass("TRN2", target_bir_lowering=False,
                   detect_race_conditions=False)

    dr = {}
    dr["x"] = nc.dram_tensor("x", [SEQ, D], F32, kind="ExternalInput")
    dr["wkqv"] = nc.dram_tensor("wkqv", [D, 3 * D], F32, kind="ExternalInput")
    dr["wr"] = nc.dram_tensor("wr", [D, D], F32, kind="ExternalInput")
    dr["wk"] = nc.dram_tensor("wk", [D, D], F32, kind="ExternalInput")
    dr["wv"] = nc.dram_tensor("wv", [D, D], F32, kind="ExternalInput")
    dr["mur"] = nc.dram_tensor("mur", [D], F32, kind="ExternalInput")
    dr["muk"] = nc.dram_tensor("muk", [D], F32, kind="ExternalInput")
    dr["identc"] = nc.dram_tensor("identc", [128, 128], BF16, kind="ExternalInput")
    dr["id8c"] = nc.dram_tensor("id8c", [8, 8], F32, kind="ExternalInput")
    dr["m1c"] = nc.dram_tensor("m1c", [128, 128], BF16, kind="ExternalInput")
    dr["m2c"] = nc.dram_tensor("m2c", [128, 128], BF16, kind="ExternalInput")
    dr["biasjc"] = nc.dram_tensor("biasjc", [128, 80], F32, kind="ExternalInput")
    dr["biasrc"] = nc.dram_tensor("biasrc", [128, 128], F32, kind="ExternalInput")
    dr["out"] = nc.dram_tensor("out", [SEQ, D], F32, kind="ExternalOutput")

    c = Ctx()
    c.nc = nc
    c.dr = dr

    with contextlib.ExitStack() as _es:
        def sb(name, shape, dtype):
            return _es.enter_context(nc.sbuf_tensor(name, shape, dtype))
        c.WKQV = sb("WKQV", [128, 8 * 3072], BF16)
        c.WTR = sb("WTR", [128, 8 * 1024], BF16)
        c.WTK = sb("WTK", [128, 8 * 1024], BF16)
        c.WTV = sb("WTV", [128, 8 * 1024], BF16)
        c.KC = sb("KC", [128, 8 * 1024], BF16)
        c.VC = sb("VC", [128, 8 * 1040], BF16)
        c.QB = sb("QB", [128, 8 * 512], BF16)
        c.XRES = sb("XRES", [128, 4 * 1024], F32)
        c.LNKA = sb("LNKA", [128, 8 * 513], BF16)
        c.XRT = sb("XRT", [128, 8 * 512], BF16)
        c.XKT = sb("XKT", [128, 8 * 512], BF16)
        c.RBUF = sb("RBUF", [128, 1024], BF16)
        c.LNB = sb("LNB", [128, 2 * 1024], BF16)
        c.SCR = sb("SCR", [128, 7168], BF16)
        c.EWS = sb("EWS", [128, 3584], BF16)
        c.MU = sb("MU", [128, 16], F32)
        c.IDENT = sb("IDENT", [128, 128], BF16)
        c.ID8 = sb("ID8", [8, 8], F32)
        c.M1 = sb("M1", [128, 128], BF16)
        c.M2 = sb("M2", [128, 128], BF16)
        c.BIASJ = sb("BIASJ", [128, 80], F32)
        c.BIASR = sb("BIASR", [128, 128], F32)
        c.EPSC = sb("EPSC", [128, 1], F32)
        c.STT = sb("STT", [128, 48], F32)
        c.RS4 = sb("RS4", [128, 8], F32)
        c.SAVE1 = sb("SAVE1", [128, 8], BF16)
        c.SAVE3 = sb("SAVE3", [128, 8], BF16)
        c.PS_S = _es.enter_context(nc.psum_tensor("PS_S", [128, 2560], F32))
        c.PS_A = _es.enter_context(nc.psum_tensor("PS_A", [128, 512], F32))
        c.PS_B = _es.enter_context(nc.psum_tensor("PS_B", [128, 512], F32))
        c.PS_X = _es.enter_context(nc.psum_tensor("PS_X", [128, 512], F32))
        sems = {}
        for sname in ("scd", "sxd0", "sxd1", "sxd2", "sxd3", "sstg0", "sstg1",
                      "po0", "po1", "po2", "po3", "pe", "act", "dve", "pool"):
            sems[sname] = _es.enter_context(nc.semaphore("s_" + sname))
        c.P = Prog(nc, sems)

        c.MUS = c.SCR[0:8, 0:512].bitcast(F32)
        c.LNT = c.LNKA[:].rearrange("p (c w) -> p c w", w=513)
        c.PT = [c.SCR[:, 0:2560], c.SCR[:, 2560:5120]]
        c.PTtok = [("scrA", "scrB"), ("scrC", "scrD")]
        c.STG = [c.SCR[:, 0:2048].bitcast(F32), c.SCR[:, 2048:4096].bitcast(F32)]
        c.STGtok = [("scrA",), ("scrB", "scrC")]
        c.WSTG = c.SCR[:, 4096:5120]
        c.SQD = c.SCR[:, 5120:7168].bitcast(F32)
        c.EW_D = c.EWS[:, 0:512]
        c.KSC = [c.EWS[:, 512:1536].bitcast(F32), c.EWS[:, 1536:2560].bitcast(F32)]
        c.KSCtok = ["ewb", "ewc"]
        c.TH = c.EWS[:, 512:1536]
        c.FV = [c.EWS[:, 1536:2560], c.EWS[:, 2560:3584]]
        c.FVtok = ["ewc", "ewf"]

        build_program(c, ns)
    return nc


def build_program(c, ns):
    P = c.P
    preamble(c)
    weights(c)
    for s in range(ns):
        phase_ln(c, s, 1)
        phase_mix_k(c, s)
        phase_rv(c, s, store=False)
        phase_ln(c, s, 2)
        phase_kqv(c, s)
        phase_attn(c, s)
        phase_ln(c, s, 3)
        phase_mix_k(c, s)
        phase_rv(c, s, store=True, refill=(s + 1 < ns))
    P.op("gpsimd", lambda g: g.engine_nop(),
         waits=[(f"po{tt}", P.cnt[f"po{tt}"]) for tt in range(4)], inc=None)
    P.flush()


def cdma(P, eng, dst, src, sem, rd=(), wr=()):
    return P.op(eng, lambda e, d=dst, s=src: e.dma_start(d, s), inc=sem,
                step=16, rd=rd, wr=wr)


def preamble(c):
    P, dr = c.P, c.dr
    for nm, tok in (("identc", "ident"), ("id8c", "id8"), ("m1c", "m1"),
                    ("m2c", "m2"), ("biasjc", "biasj"), ("biasrc", "biasr")):
        dst = {"ident": c.IDENT, "id8": c.ID8, "m1": c.M1, "m2": c.M2,
               "biasj": c.BIASJ, "biasr": c.BIASR}[tok]
        cdma(P, "sync", dst[:], dr[nm].ap(), "scd", wr=(tok,))
    cdma(P, "sync", c.MUS[0:8, 0:128],
         dr["mur"].ap().rearrange("(c p) -> c p", p=128), "scd", wr=("scrA",))
    cdma(P, "sync", c.MUS[0:8, 128:256],
         dr["muk"].ap().rearrange("(c p) -> c p", p=128), "scd", wr=("scrA",))
    for tt in range(4):
        cdma(P, "sync", c.XRES[:, tt * 1024:(tt + 1) * 1024],
             dr["x"].ap()[tt * 128:(tt + 1) * 128, :], f"sxd{tt}", wr=(f"x{tt}",))
    P.op("vector", lambda v: v.memset(c.EPSC[:], EPS), wr=("epsc",))
    vco = c.VC[:].rearrange("p (g w) -> p g w", w=65)
    P.op("vector", lambda v: v.memset(vco[:, :, 0:1], 1.0), wr=("vc",))
    P.op("vector", lambda v: v.memset(c.LNT[:, :, 0:1], 0.0), wr=("lnt",))
    P.op("vector", lambda v: v.memset(c.SAVE1[:], 0.0), wr=("save1",))
    P.op("vector", lambda v: v.memset(c.SAVE3[:], 0.0), wr=("save3",))
    P.op("tensor", lambda t: t.matmul(
        c.PS_A[:, 0:8], c.MUS[0:8, 0:128], c.ID8[:], start=True, stop=True),
        rd=("scrA", "id8"), wr=("ps5",))
    P.op("tensor", lambda t: t.matmul(
        c.PS_A[:, 8:16], c.MUS[0:8, 128:256], c.ID8[:], start=True, stop=True),
        rd=("scrA", "id8"), wr=("ps5",))
    P.op("vector", lambda v: v.tensor_copy(c.MU[:], c.PS_A[:, 0:16]),
         rd=("ps5",), wr=("mu",))


def weights(c):
    P, dr = c.P, c.dr
    wd = {0: dr["wr"], 1: dr["wk"], 2: dr["wv"]}
    wt = {0: (c.WTR, "wtr"), 1: (c.WTK, "wtk"), 2: (c.WTV, "wtv")}
    jobs = ([("kqv", cc, t3) for cc in range(8) for t3 in range(3)] +
            [("ffn", w, dj) for w in range(3) for dj in range(8)])
    m = 0
    for n, job in enumerate(jobs):
        kind = job[0]
        stok = c.STGtok[n % 2]
        if kind == "kqv":
            _, cc, t3 = job
            src = dr["wkqv"].ap()[cc * 128:(cc + 1) * 128, t3 * 1024:(t3 + 1) * 1024]
        else:
            _, w, dj = job
            src = wd[w].ap()[dj * 128:(dj + 1) * 128, :]
        cdma(P, "sync", c.STG[n % 2][:], src, f"sstg{n % 2}", wr=stok)
        if kind == "kqv":
            _, cc, t3 = job
            dst = c.WKQV[:, cc * 3072 + t3 * 1024: cc * 3072 + (t3 + 1) * 1024]
            if t3 == 1:
                P.op("vector", lambda v, d_=dst, s_=c.STG[n % 2]:
                     v.tensor_scalar_mul(d_, s_[:], 1.0 / 32.0),
                     rd=stok, wr=("wkqv",))
            else:
                P.op("vector", lambda v, d_=dst, s_=c.STG[n % 2]:
                     v.tensor_copy(d_, s_[:]), rd=stok, wr=("wkqv",))
        else:
            _, w, dj = job
            if w == 2:
                P.op("vector", lambda v, s_=c.STG[n % 2]:
                     v.tensor_scalar_mul(c.WSTG[:], s_[:], 0.5),
                     rd=stok, wr=("scrD",))
            else:
                P.op("vector", lambda v, s_=c.STG[n % 2]:
                     v.tensor_copy(c.WSTG[:], s_[:]), rd=stok, wr=("scrD",))
            ps = c.PS_S[:, (m % 2) * 1024:(m % 2 + 1) * 1024]
            ptoks = sbanks((m % 2) * 1024, 1024)
            for ci in range(8):
                P.op("tensor", lambda t, ps_=ps, ci_=ci:
                     t.matmul(ps_[:, ci_ * 128:(ci_ + 1) * 128],
                              c.WSTG[:, ci_ * 128:(ci_ + 1) * 128],
                              c.IDENT[:], start=True, stop=True),
                     rd=("scrD", "ident"), wr=ptoks,
                     inc="eng" if ci == 7 else None)
            src2 = ps.rearrange("p (ci w2) -> p ci w2", w2=128)
            dst2 = wt[w][0][:].rearrange("p (ci w2) -> p ci w2", w2=1024)[:, :, dj * 128:(dj + 1) * 128]
            P.op("scalar", lambda a, d_=dst2, s_=src2: a.copy(d_, s_),
                 rd=ptoks, wr=(wt[w][1],))
            m += 1
    P.flush()


def ln_tile_stats(c, tt):
    P = c.P
    ST = c.STT
    xk = f"x{tt}"
    xres = c.XRES[:, tt * 1024:(tt + 1) * 1024]
    P.op("scalar", lambda a, x_=xres, tt_=tt: a.activation(
        c.SQD[:], x_, AF.Square, accum_out=ST[:, 4 + tt_:5 + tt_]),
        rd=(xk,), wr=("scrE", f"sttS{tt}"))
    P.op("vector", lambda v, x_=xres, tt_=tt: v.reduce_sum(
        ST[:, tt_:tt_ + 1], x_, axis=AX.X), rd=(xk,), wr=(f"sttR{tt}",))


def phase_ln(c, s, which, skip_stats=False):
    P = c.P
    save, stok = {1: (c.SAVE1, "save1"), 2: (None, None),
                  3: (c.SAVE3, "save3")}[which]
    if which != 2 and s > 0:
        P.op("vector", lambda v, sv=save: v.tensor_copy(
            c.LNT[:, :, 0:1], sv[:].rearrange("p (cc w) -> p cc w", w=1)),
            rd=(stok,), wr=("lnt",))
    ST = c.STT
    if not skip_stats:
        for tt in range(4):
            ln_tile_stats(c, tt)
    sall = ("sttS0", "sttS1", "sttS2", "sttS3",
            "sttR0", "sttR1", "sttR2", "sttR3")
    P.op("vector", lambda v: v.scalar_tensor_tensor(
        ST[:, 8:12], ST[:, 0:4], -1.0 / (D * D), ST[:, 0:4],
        ALU.mult, ALU.mult), rd=sall, wr=("sttF",))
    P.op("vector", lambda v: v.scalar_tensor_tensor(
        ST[:, 12:16], ST[:, 4:8], 1.0 / D, ST[:, 8:12],
        ALU.mult, ALU.add), rd=sall + ("sttF",), wr=("sttF",))
    P.op("scalar", lambda a: a.activation(
        ST[:, 16:20], ST[:, 12:16], AF.Sqrt, bias=c.EPSC[:]),
        rd=("sttF", "epsc"), wr=("sttF",))
    P.op("vector", lambda v: v.reciprocal(
        ST[:, 20:24], ST[:, 16:20]), rd=("sttF",), wr=("sttF",))
    P.op("vector", lambda v: v.scalar_tensor_tensor(
        ST[:, 24:28], ST[:, 0:4], -1.0 / D, ST[:, 20:24],
        ALU.mult, ALU.mult), rd=sall + ("sttF",), wr=("sttF",))
    for tt in range(4):
        xk = f"x{tt}"
        xres = c.XRES[:, tt * 1024:(tt + 1) * 1024]
        ln = c.LNB[:, (tt % 2) * 1024:(tt % 2 + 1) * 1024]
        lt = f"lnb{tt % 2}"
        P.op("scalar", lambda a, x_=xres, tt_=tt, ln_=ln: a.activation(
            ln_, x_, AF.Identity, bias=ST[:, 24 + tt_:25 + tt_],
            scale=ST[:, 20 + tt_:21 + tt_]),
            rd=(xk, "sttF"), wr=(lt,))
        ps = c.PS_S[:, (tt % 2) * 1024:(tt % 2 + 1) * 1024]
        ptoks = sbanks((tt % 2) * 1024, 1024)
        for ci in range(8):
            P.op("tensor", lambda t, ps_=ps, ln_=ln, ci_=ci:
                 t.matmul(ps_[:, ci_ * 128:(ci_ + 1) * 128],
                          ln_[:, ci_ * 128:(ci_ + 1) * 128],
                          c.IDENT[:], start=True, stop=True),
                 rd=(lt, "ident"), wr=ptoks, inc="eng" if ci == 7 else None)
        src = ps.rearrange("p (ci w) -> p ci w", w=128)
        dst = c.LNT[:, :, 1 + tt * 128: 1 + (tt + 1) * 128]
        P.op("scalar", lambda a, d_=dst, s_=src: a.copy(d_, s_),
             rd=ptoks, wr=("lnt",))
    if save is not None:
        P.op("vector", lambda v, sv=save: v.tensor_copy(
            sv[:].rearrange("p (cc w) -> p cc w", w=1), c.LNT[:, :, 512:513]),
            rd=("lnt",), wr=(stok,))


def phase_mix_k(c, s):
    P = c.P
    d3 = c.XKT[:].rearrange("p (cc w) -> p cc w", w=512)
    xtoks = tuple(f"xkt{cc}" for cc in range(8))
    P.op("vector", lambda v: v.tensor_sub(
        d3, c.LNT[:, :, 1:513], c.LNT[:, :, 0:512]), rd=("lnt",), wr=xtoks)
    for cc in range(8):
        sh = c.LNT[:, cc, 0:512]
        P.op("vector", lambda v, sh_=sh, cc_=cc: v.scalar_tensor_tensor(
            c.XRT[:, cc_ * 512:(cc_ + 1) * 512],
            c.XKT[:, cc_ * 512:(cc_ + 1) * 512],
            c.MU[:, cc_:cc_ + 1], sh_, ALU.mult, ALU.add),
            rd=(f"xkt{cc}", "lnt", "mu"), wr=("xrt",))
        P.op("vector", lambda v, sh_=sh, cc_=cc: v.scalar_tensor_tensor(
            c.XKT[:, cc_ * 512:(cc_ + 1) * 512],
            c.XKT[:, cc_ * 512:(cc_ + 1) * 512],
            c.MU[:, 8 + cc_:9 + cc_], sh_, ALU.mult, ALU.add),
            rd=(f"xkt{cc}", "lnt", "mu"), wr=(f"xkt{cc}",))
    for dp in range(8):
        ps = (c.PS_A if dp % 2 == 0 else c.PS_B)[:, 0:512]
        btok = "ps5" if dp % 2 == 0 else "ps6"
        for cc in range(8):
            rdt = (("wtk",) + tuple(f"xkt{i}" for i in range(8))) if cc == 7 \
                else ("wtk", f"xkt{cc}")
            P.op("tensor", lambda t, ps_=ps, cc_=cc, dp_=dp: t.matmul(
                ps_, c.WTK[:, cc_ * 1024 + dp_ * 128: cc_ * 1024 + (dp_ + 1) * 128],
                c.XKT[:, cc_ * 512:(cc_ + 1) * 512],
                start=(cc_ == 0), stop=(cc_ == 7)),
                rd=rdt, wr=(btok,), inc="eng" if cc == 7 else None)
        kt = c.KSCtok[dp % 2]
        P.op("scalar", lambda a, ps_=ps, dp_=dp: a.activation(
            c.KSC[dp_ % 2][:, 0:512], ps_, AF.Relu), rd=(btok,), wr=(kt,))
        P.op("vector", lambda v, dp_=dp: v.tensor_mul(
            c.LNT[:, dp_, 1:513], c.KSC[dp_ % 2][:, 0:512],
            c.KSC[dp_ % 2][:, 0:512]), rd=(kt,), wr=("lnt",))


def phase_rv(c, s, store, refill=False, post_tile=None):
    P, dr = c.P, c.dr

    def regions(tt):
        if tt % 2 == 0:
            return ((c.PS_S[:, 0:512], "ps0"), (c.PS_S[:, 512:1024], "ps1"),
                    (c.PS_S[:, 1024:1536], "ps2"), (c.PS_S[:, 1536:2048], "ps3"))
        return ((c.PS_A[:, 0:512], "ps5"), (c.PS_B[:, 0:512], "ps6"),
                (c.PS_S[:, 2048:2560], "ps4"), (c.PS_X[:, 0:512], "ps7"))

    for tt in range(4):
        (r0, r0t), (r1, r1t), (v0, v0t), (v1, v1t) = regions(tt)
        for n, (ps, pt) in enumerate(((r0, r0t), (r1, r1t))):
            for cc in range(8):
                P.op("tensor", lambda t, ps_=ps, cc_=cc, n_=n, tt_=tt:
                     t.matmul(ps_,
                              c.XRT[:, cc_ * 512 + tt_ * 128: cc_ * 512 + tt_ * 128 + 128],
                              c.WTR[:, cc_ * 1024 + n_ * 512: cc_ * 1024 + (n_ + 1) * 512],
                              start=(cc_ == 0), stop=(cc_ == 7)),
                     rd=("xrt", "wtr"), wr=(pt,), inc="eng" if cc == 7 else None)
        for n, (ps, pt) in enumerate(((v0, v0t), (v1, v1t))):
            for cc in range(8):
                P.op("tensor", lambda t, ps_=ps, cc_=cc, n_=n, tt_=tt:
                     t.matmul(ps_,
                              c.LNT[:, cc_, 1 + tt_ * 128: 1 + tt_ * 128 + 128],
                              c.WTV[:, cc_ * 1024 + n_ * 512: cc_ * 1024 + (n_ + 1) * 512],
                              start=(cc_ == 0), stop=(cc_ == 7)),
                     rd=("lnt", "wtv"), wr=(pt,), inc="eng" if cc == 7 else None)
        P.op("scalar", lambda a, r0_=r0: a.activation(
            c.TH[:, 0:512], r0_, AF.Tanh, scale=0.5), rd=(r0t,), wr=("ewb",))
        P.op("scalar", lambda a, r1_=r1: a.activation(
            c.TH[:, 512:1024], r1_, AF.Tanh, scale=0.5), rd=(r1t,), wr=("ewb",))
        P.op("vector", lambda v: v.tensor_scalar(
            c.RBUF[:], c.TH[:], 0.5, 0.5, ALU.mult, ALU.add),
            rd=("ewb",), wr=("rbuf",))
        fv = c.FV[tt % 2]
        ft = c.FVtok[tt % 2]
        P.op("vector", lambda v, v0_=v0, fv_=fv: v.tensor_mul(
            fv_[:, 0:512], c.RBUF[:, 0:512], v0_), rd=("rbuf", v0t), wr=(ft,))
        P.op("vector", lambda v, v1_=v1, fv_=fv: v.tensor_mul(
            fv_[:, 512:1024], c.RBUF[:, 512:1024], v1_), rd=("rbuf", v1t), wr=(ft,))
        xres = c.XRES[:, tt * 1024:(tt + 1) * 1024]
        xk = f"x{tt}"
        P.op("vector", lambda g, x_=xres, fv_=fv: g.tensor_add(
            x_, x_, fv_[:]), rd=(xk, ft), wr=(xk,))
        if store:
            T = 4 * s + tt
            P.op("gpsimd", lambda g, x_=xres, T_=T: g.dma_start(
                dr["out"].ap()[T_ * 128:(T_ + 1) * 128, :], x_),
                inc=f"po{tt}", step=16, rd=(xk,))
            if refill:
                Tn = T + 4
                P.op("sync", lambda e, x_=xres, Tn_=Tn: e.dma_start(
                    x_, dr["x"].ap()[Tn_ * 128:(Tn_ + 1) * 128, :]),
                    inc=f"sxd{tt}", step=16, wr=(xk,))
        if post_tile is not None:
            post_tile(tt)


def phase_kqv(c, s):
    P = c.P
    for part in range(2):
        for j in range(8):
            a0 = (part * 2 + j % 2) * 512
            ps = c.PS_S[:, a0:a0 + 512]
            btok = sbanks(a0, 512)
            for cc in range(8):
                P.op("tensor", lambda t, ps_=ps, cc_=cc, j_=j, p_=part:
                     t.matmul(ps_,
                              c.WKQV[:, cc_ * 3072 + p_ * 1024 + j_ * 128:
                                     cc_ * 3072 + p_ * 1024 + (j_ + 1) * 128],
                              c.LNT[:, cc_, 1:513],
                              start=(cc_ == 0), stop=(cc_ == 7)),
                     rd=("wkqv", "lnt"), wr=btok, inc="eng" if cc == 7 else None)
            if part == 0:
                dst = c.KC[:, j * 1024 + (s % 2) * 512: j * 1024 + (s % 2) * 512 + 512]
                dtok = "kc"
            else:
                dst = c.QB[:, j * 512:(j + 1) * 512]
                dtok = "qb"
            P.op("scalar", lambda a, d_=dst, ps_=ps: a.copy(d_, ps_),
                 rd=btok, wr=(dtok,))
    for tt in range(4):
        slot = (4 * s + tt) % 8
        pss = (((c.PS_S[:, 2048:2560], "ps4"), (c.PS_X[:, 0:512], "ps7"))
               if tt % 2 == 0 else
               ((c.PS_A[:, 0:512], "ps5"), (c.PS_B[:, 0:512], "ps6")))
        vcs = c.VC[:, slot * 1040:(slot + 1) * 1040].rearrange("p (h w) -> p h w", w=65)
        for n, (ps, pt) in enumerate(pss):
            for cc in range(8):
                P.op("tensor", lambda t, ps_=ps, cc_=cc, n_=n, tt_=tt:
                     t.matmul(ps_,
                              c.LNT[:, cc_, 1 + tt_ * 128: 1 + tt_ * 128 + 128],
                              c.WKQV[:, cc_ * 3072 + 2048 + n_ * 512:
                                     cc_ * 3072 + 2048 + (n_ + 1) * 512],
                              start=(cc_ == 0), stop=(cc_ == 7)),
                     rd=("lnt", "wkqv"), wr=(pt,), inc="eng" if cc == 7 else None)
            src = ps.rearrange("p (h w) -> p h w", w=64)
            P.op("vector", lambda v, d_=vcs[:, 8 * n:8 * n + 8, 1:65], s_=src:
                 v.tensor_copy(d_, s_), rd=(pt,), wr=("vc",))


def attn_schedule(s):
    groups = []
    for kt in range(4 * s - 4, 4 * s + 4):
        if kt < 0:
            continue
        qlo = max(kt, 4 * s)
        qhi = min(kt + 4, 4 * s + 3)
        groups.append((kt, qlo, qhi))
    offs = []
    o = 0
    for (kt, qlo, qhi) in groups:
        offs.append(o)
        o += (qhi - qlo + 1) * 128
    return groups, offs


def _skip_slice(h, jj):
    return h < BATCHED_EXP_H0 and SLOPES[h] * (128 * jj - 63) > 105.0


def phase_attn(c, s):
    P = c.P
    groups, offs = attn_schedule(s)

    def emit_s(h):
        j = h >> 1
        half = (h & 1) * 64
        pieces = []
        for gi, (kt, qlo, qhi) in enumerate(groups):
            w = (qhi - qlo + 1) * 128
            a = offs[gi]
            while w > 0:
                wp = min(w, 512 - a % 512)
                pieces.append((kt, qlo, a, a - offs[gi], wp))
                a += wp
                w -= wp
        for pi, (kt, qlo, a, rel, wp) in enumerate(pieces):
            slot = kt % 8
            P.op("tensor", lambda t, a_=a, rel_=rel, wp_=wp, qlo_=qlo, slot_=slot:
                 t.matmul(c.PS_S[:, a_:a_ + wp_],
                          c.KC[half:half + 64,
                               j * 1024 + slot_ * 128: j * 1024 + (slot_ + 1) * 128],
                          c.QB[half:half + 64,
                               j * 512 + (qlo_ - 4 * s) * 128 + rel_:
                               j * 512 + (qlo_ - 4 * s) * 128 + rel_ + wp_],
                          start=True, stop=True, tile_position=(half, 0)),
                 rd=("kc", "qb"), wr=sbanks(a, wp),
                 inc="eng" if pi == len(pieces) - 1 else None)

    def emit_exp(h):
        pt = c.PT[h % 2]
        ptok = c.PTtok[h % 2]
        for gi, (kt, qlo, qhi) in enumerate(groups):
            if h < BATCHED_EXP_H0:
                for tt in range(qlo - 4 * s, qhi - 4 * s + 1):
                    jj = (4 * s + tt) - kt
                    if _skip_slice(h, jj):
                        continue
                    o = offs[gi] + (tt - (qlo - 4 * s)) * 128
                    P.op("scalar", lambda a, o_=o, h_=h, jj_=jj: a.activation(
                        c.PT[h_ % 2][:, o_:o_ + 128], c.PS_S[:, o_:o_ + 128],
                        AF.Exp, bias=c.BIASJ[:, h_ * 5 + jj_: h_ * 5 + jj_ + 1]),
                        rd=sbanks(o, 128) + ("biasj",), wr=ptok)
            else:
                r = kt - 4 * s + 4
                w = (qhi - qlo + 1) * 128
                P.op("scalar", lambda a, gi_=gi, w_=w, h_=h, r_=r: a.activation(
                    c.PT[h_ % 2][:, offs[gi_]:offs[gi_] + w_],
                    c.PS_S[:, offs[gi_]:offs[gi_] + w_], AF.Exp,
                    bias=c.BIASR[:, h_ * 8 + r_: h_ * 8 + r_ + 1]),
                    rd=sbanks(offs[gi], w) + ("biasr",), wr=ptok)

    def emit_masks(h):
        ptok = c.PTtok[h % 2]
        for gi, (kt, qlo, qhi) in enumerate(groups):
            if kt < 4 * s and kt + 4 <= 4 * s + 3 and not _skip_slice(h, 4):
                o = offs[gi] + (kt + 4 - qlo) * 128
                P.op("vector", lambda g, o_=o, h_=h: g.tensor_mul(
                    c.PT[h_ % 2][:, o_:o_ + 128], c.PT[h_ % 2][:, o_:o_ + 128],
                    c.M1[:]), rd=ptok + ("m1",), wr=ptok)
            if kt >= 4 * s:
                o = offs[gi]
                P.op("vector", lambda g, o_=o, h_=h: g.tensor_mul(
                    c.PT[h_ % 2][:, o_:o_ + 128], c.PT[h_ % 2][:, o_:o_ + 128],
                    c.M2[:]), rd=ptok + ("m2",), wr=ptok)

    def emit_av(h):
        ps = c.PS_A if h % 2 == 0 else c.PS_B
        btok = "ps5" if h % 2 == 0 else "ps6"
        ptok = c.PTtok[h % 2]
        ops = []
        for tt in range(4):
            tq = 4 * s + tt
            kts = [g for g in groups
                   if g[1] <= tq <= g[2] and not _skip_slice(h, tq - g[0])]
            for i, (kt, qlo, qhi) in enumerate(kts):
                gi = groups.index((kt, qlo, qhi))
                o = offs[gi] + (tq - qlo) * 128
                ops.append((tt, o, kt % 8, i == 0, i == len(kts) - 1))
        for i, (tt, o, slot, st_, sp_) in enumerate(ops):
            P.op("tensor", lambda t, o_=o, tt_=tt, slot_=slot, h_=h,
                 s0=st_, s1=sp_, ps_=ps:
                 t.matmul(ps_[:, tt_ * 65: tt_ * 65 + 65],
                          c.PT[h_ % 2][:, o_:o_ + 128],
                          c.VC[:, slot_ * 1040 + h_ * 65: slot_ * 1040 + (h_ + 1) * 65],
                          start=s0, stop=s1),
                 rd=ptok + ("vc",), wr=(btok,),
                 inc="eng" if i == len(ops) - 1 else None)

    def emit_stt(h):
        ps = (c.PS_A if h % 2 == 0 else c.PS_B)[:, 0:512]
        btok = "ps5" if h % 2 == 0 else "ps6"
        pv = ps[:, 0:260].rearrange("p (q w) -> p q w", w=65)
        rs = c.RS4[:, (h % 2) * 4:(h % 2) * 4 + 4]
        P.op("vector", lambda v, rs_=rs, pv_=pv: v.reciprocal(
            rs_.rearrange("p (q w) -> p q w", w=1), pv_[:, 0:4, 0:1]),
            rd=(btok,), wr=("rs4",))
        for tt in range(4):
            xk = f"x{tt}"
            P.op("vector", lambda v, tt_=tt, h_=h, ps_=ps, rs_=rs:
                 v.scalar_tensor_tensor(
                     c.XRES[:, tt_ * 1024 + h_ * 64: tt_ * 1024 + h_ * 64 + 64],
                     ps_[:, tt_ * 65 + 1: tt_ * 65 + 65],
                     rs_[:, tt_:tt_ + 1],
                     c.XRES[:, tt_ * 1024 + h_ * 64: tt_ * 1024 + h_ * 64 + 64],
                     ALU.mult, ALU.add),
                 rd=(btok, "rs4", xk), wr=(xk,), no_self_raw=(xk,))

    for h in range(H):
        emit_s(h)
        emit_exp(h)
        emit_masks(h)
        if h >= 1:
            emit_av(h - 1)
            emit_stt(h - 1)
    emit_av(H - 1)
    emit_stt(H - 1)


def kernel(L0_attn_x, L0_attn_w_kqv, L0_ffn_W_r, L0_ffn_W_k, L0_ffn_W_v,
           L0_ffn_mu_r, L0_ffn_mu_k):
    ident, id8, m1, m2, biasj, biasr = _host_consts()
    nc = build(NS)
    base = {
        "wkqv": np.ascontiguousarray(np.asarray(L0_attn_w_kqv, np.float32)),
        "wr": np.ascontiguousarray(np.asarray(L0_ffn_W_r, np.float32)),
        "wk": np.ascontiguousarray(np.asarray(L0_ffn_W_k, np.float32)),
        "wv": np.ascontiguousarray(np.asarray(L0_ffn_W_v, np.float32)),
        "mur": np.ascontiguousarray(np.asarray(L0_ffn_mu_r, np.float32)),
        "muk": np.ascontiguousarray(np.asarray(L0_ffn_mu_k, np.float32)),
        "identc": ident, "id8c": id8, "m1c": m1, "m2c": m2,
        "biasjc": biasj, "biasrc": biasr,
    }
    x_full = np.asarray(L0_attn_x, np.float32)
    in_maps = []
    for b in range(8):
        m = dict(base)
        m["x"] = np.ascontiguousarray(x_full[b])
        in_maps.append(m)
    res = run_bass_kernel_spmd(nc, in_maps, core_ids=list(range(8)))
    out = np.stack([np.asarray(res.results[b]["out"], np.float32)
                    for b in range(8)], axis=0)
    return out
